# version 38
# baseline (speedup 1.0000x reference)
import numpy as np
import ml_dtypes

import concourse.bass as bass
import concourse.mybir as mybir
from concourse.bass_utils import run_bass_kernel_spmd

F32 = mybir.dt.float32
BF16 = mybir.dt.bfloat16
F8 = mybir.dt.float8e4
AF = mybir.ActivationFunctionType
ALU = mybir.AluOpType

N_CORES = 8
B_FULL = 16
T_SEQ = 2048
C = 1024
H = 16
D = 64
QT = 128
EPS = 1e-5
RING = 4

B_CORE = B_FULL // N_CORES
NT = T_SEQ // QT


def _qkv_row_perm():
    rows = []
    for base, par in ((0, 0), (0, 1), (64, 0), (64, 1)):
        for h in range(H):
            for j in range(32):
                rows.append(h * 192 + base + 2 * j + par)
    for h in range(H):
        for d in range(D):
            rows.append(h * 192 + 128 + d)
    assert len(rows) == 3 * C and len(set(rows)) == 3 * C
    return np.asarray(rows, dtype=np.int64)


def _rope_tables(t_seq):
    d2 = D // 2
    theta = (10000.0 ** (-(np.arange(d2, dtype=np.float32) / np.float32(d2)))).astype(np.float32)
    ang32 = np.arange(t_seq, dtype=np.float32)[:, None] * theta[None, :]
    cosv = np.cos(ang32.astype(np.float64)).astype(np.float32)
    sinv = np.sin(ang32.astype(np.float64)).astype(np.float32)
    p = np.arange(128) % d2
    return (np.ascontiguousarray(cosv[:, p].T).astype(ml_dtypes.bfloat16),
            np.ascontiguousarray(sinv[:, p].T).astype(ml_dtypes.bfloat16))


class _Builder:
    def __init__(self, b_core, nt, debug=False):
        self.debug = debug
        self.b_core, self.nt = b_core, nt
        self.t_seq = nt * QT
        self.n_tiles = b_core * nt
        self.queues = {k: [] for k in ("sync", "gpsimd", "vector", "scalar", "tensor")}
        self.counts = {k: 0 for k in ("x", "tx", "ty", "wg", "wh", "wi", "wj", "wo", "wf", "st", "t", "v", "s", "g")}
        self.miles = {}
        self.sems = {}

    def emit(self, eng, fn):
        self.queues[eng].append(fn)

    def wait(self, eng, milestone):
        sem_name, val = self.miles[milestone]
        self.emit(eng, lambda e, sn=sem_name, v=val: e.wait_ge(self.sems[sn], v))

    def step(self, eng, sem_name, milestone, fn, by=1):
        def run(e, f=fn, sn=sem_name, b=by):
            ins = f(e)
            ins.then_inc(self.sems[sn], b)
        self.emit(eng, run)
        self.counts[sem_name] += by
        if milestone is not None:
            self.miles[milestone] = (sem_name, self.counts[sem_name])

    def dma_step(self, eng, sem_name, milestone, fn):
        prev = self.counts[sem_name]
        if prev:
            self.emit(eng, lambda e, sn=sem_name, v=prev: e.wait_ge(self.sems[sn], v))
        self.step(eng, sem_name, milestone, fn, by=16)

    def build(self):
        nc = bass.Bass("TRN2", target_bir_lowering=False)
        self.nc = nc
        t_seq, nt = self.t_seq, self.nt

        self.x_h = nc.dram_tensor("x", [self.b_core * t_seq, C], F32, kind="ExternalInput")
        self.wqkv_h = nc.dram_tensor("wqkvT", [C, 3 * C], F32, kind="ExternalInput")
        self.wout_h = nc.dram_tensor("woutT", [C, C], F32, kind="ExternalInput")
        self.wffn_h = nc.dram_tensor("wffnT", [C, C], F32, kind="ExternalInput")
        self.bout_h = nc.dram_tensor("boutB", [128, C], F32, kind="ExternalInput")
        self.bffn_h = nc.dram_tensor("bffnB", [128, C], F32, kind="ExternalInput")
        self.cos_h = nc.dram_tensor("ropecos", [128, t_seq], BF16, kind="ExternalInput")
        self.sin_h = nc.dram_tensor("ropesin", [128, t_seq], BF16, kind="ExternalInput")
        self.mask3_h = nc.dram_tensor("mask3", [128, 384], BF16, kind="ExternalInput")
        self.idbf_h = nc.dram_tensor("identbf", [128, 128], BF16, kind="ExternalInput")
        self.out_h = nc.dram_tensor("out", [self.b_core * t_seq, C], F32, kind="ExternalOutput")

        from contextlib import ExitStack
        es = ExitStack()
        sbuf = lambda n, sh, dt: es.enter_context(nc.sbuf_tensor(n, sh, dt))

        self.wqkvT = sbuf("wqkvT_sb", [128, 8 * 3 * C], BF16)
        self.woutT = sbuf("woutT_sb", [128, 8 * C], BF16)
        self.wffnT = sbuf("wffnT_sb", [128, 8 * C], BF16)
        self.wf8 = sbuf("wf8_sb", [128, C], F8)
        self.wstg = [sbuf(f"wstg{i}", [128, C], F32) for i in range(4)]
        self.cos_sb = sbuf("cos_sb", [128, t_seq], BF16)
        self.sin_sb = sbuf("sin_sb", [128, t_seq], BF16)
        self.boutB = sbuf("boutB_sb", [128, C], F32)
        self.bffnB = sbuf("bffnB_sb", [128, C], F32)
        self.mask3 = sbuf("mask3_sb", [128, 384], BF16)
        self.idbf = sbuf("idbf_sb", [128, 128], BF16)
        self.x_in = [sbuf(f"x_in{i}", [128, C], F32) for i in range(2)]
        self.x3_out = [sbuf(f"x3_out{i}", [128, C], F32) for i in range(2)]
        self.ln1_bf = sbuf("ln1_bf", [128, C], BF16)
        self.ln1T = sbuf("ln1T", [128, C], BF16)
        self.qraw = [sbuf(f"qraw{i}", [128, 512], BF16) for i in range(4)]
        self.rp_t = [sbuf(f"rp_t{i}", [128, 512], BF16) for i in range(2)]
        self.qrx = [sbuf(f"qrx{i}", [128, 512], BF16) for i in range(2)]
        self.krx = [sbuf(f"krx{i}", [128, RING * 512], BF16) for i in range(2)]
        self.vaug = sbuf("vaug", [128, RING * 16 * 65], BF16)
        self.expT = sbuf("expT", [128, 3 * H * 128], BF16)
        self.attn_out = sbuf("attn_out", [128, C], BF16)
        self.attn_outT = sbuf("attn_outT", [128, C], BF16)
        self.rd_sb = sbuf("rd_sb", [128, H], F32)
        self.proj_b = sbuf("proj_b", [128, C], F32)
        self.t_pa = sbuf("t_pa", [128, C], F32)
        self.x2_sb = sbuf("x2_sb", [128, C], F32)
        self.ln2_f8 = sbuf("ln2_f8", [128, C], F8)
        self.ln2_bf = sbuf("ln2_bf", [128, C], BF16)
        self.ln2T = sbuf("ln2T_bf", [128, C], BF16)
        self.ffn_b = sbuf("ffn_b", [128, C], F32)
        self.t_ffn = sbuf("t_ffn", [128, C], F32)
        self.sq_scr = sbuf("sq_scr", [128, C], BF16)
        self.lnscr = sbuf("lnscr", [128, 4 * 2 * 8], F32)
        self.dbg_expin = sbuf("dbg_expin", [128, 3 * 128], F32)

        self.ps_all = es.enter_context(nc.psum_tensor("ps_all", [128, 4096], F32))
        for n in ("x", "tx", "ty", "wg", "wh", "wi", "wj", "wo", "wf", "st", "t", "v", "s", "g"):
            self.sems[n] = es.enter_context(nc.semaphore(f"sem_{n}"))

        c0 = self._ctx(0)
        self.pending_ffn = None
        self._prologue(c0)
        self._emit_qkv(c0)
        self._emit_rope(c0)
        self._pump(4)
        for gi in range(self.n_tiles):
            c = self._ctx(gi)
            nxt = self._ctx(gi + 1) if gi + 1 < self.n_tiles else None
            self._emit_attention(c, nxt)
            self._emit_tail(c, nxt)
        if self.debug:
            self._emit_debug_taps()
        total_st = self.counts["st"]
        self.emit("gpsimd", lambda e, v=total_st: e.wait_ge(self.sems["st"], v))

        with es, nc.Block() as block:
            @block.sync
            def _(eng):
                for fn in self.queues["sync"]:
                    fn(eng)

            @block.gpsimd
            def _(eng):
                for fn in self.queues["gpsimd"]:
                    fn(eng)

            @block.vector
            def _(eng):
                for fn in self.queues["vector"]:
                    fn(eng)

            @block.scalar
            def _(eng):
                for fn in self.queues["scalar"]:
                    fn(eng)

            @block.tensor
            def _(eng):
                for fn in self.queues["tensor"]:
                    fn(eng)
        return nc

    def bank(self, b, lo=0, hi=512):
        return self.ps_all[:, b * 512 + lo: b * 512 + hi]

    def _prologue(self, c0):
        step, wait = self.step, self.wait

        def ld(eng, sem, dst, src, mile=None):
            self.dma_step(eng, sem, mile,
                 lambda e, d=dst, s=src: e.dma_start(out=d[:], in_=s[:, :]))
        ld("sync", "x", self.x_in[0], self.x_h[0:128], "x_tile0")
        if self.n_tiles > 1:
            ld("sync", "x", self.x_in[1], self.x_h[128:256], "x_tile1")
        ln1_done0 = self._emit_ln1(c0)
        self._emit_ln1T(c0, ln1_done0)
        for dst, src in ((self.cos_sb, self.cos_h), (self.sin_sb, self.sin_h),
                         (self.idbf, self.idbf_h), (self.mask3, self.mask3_h)):
            ld("scalar", "wh", dst, src)
        self.miles["tables"] = ("wh", self.counts["wh"])

        jobs = []
        for hf in range(3):
            for cch in range(8):
                src = self.wqkv_h[cch * 128:(cch + 1) * 128, hf * C:(hf + 1) * C]
                jobs.append(("gpsimd", f"wqkv{cch}_{hf}", src, cch * 3 * C + hf * C))
        qsems = {"gpsimd": ("wg", "wi"), "sync": ("wh", "wj")}
        qn = {"gpsimd": 0, "sync": 0}
        for idx, (q, name, src, off) in enumerate(jobs):
            if idx >= 4:
                wait(q, f"wcast{idx - 4}")
            sem = qsems[q][qn[q] % 2]
            qn[q] += 1
            self.dma_step(q, sem, name,
                          lambda e, s_=src, b_=idx % 4: e.dma_start(
                              out=self.wstg[b_][:, :], in_=s_))
            wait("vector", name)

            def v_cast(e, o_=off, b_=idx % 4):
                e.drain()
                return e.tensor_copy(self.wqkvT[:, o_:o_ + C], self.wstg[b_][:, :])
            step("vector", "v", f"wcast{idx}", v_cast)
        sem = qsems["gpsimd"][qn["gpsimd"] % 2]
        self.dma_step("gpsimd", sem, "boutld",
                      lambda e: e.dma_start(out=self.boutB[:], in_=self.bout_h[:, :]))
        sem = qsems["gpsimd"][(qn["gpsimd"] + 1) % 2]
        self.dma_step("gpsimd", sem, "bffnld",
                      lambda e: e.dma_start(out=self.bffnB[:], in_=self.bffn_h[:, :]))

        def v_prologue(e):
            ones = self.vaug[:].rearrange("p (r h c) -> p r h c", r=RING, h=16)
            return e.memset(ones[:, :, :, 64:65], 1.0)
        step("vector", "v", "v_prologue", v_prologue)

        def mk_load_wo(m):
            def go():
                self.wait("sync", f"wcast{20 + m}" if m < 4 else f"wocast{m - 4}")
                sem = ("wh", "wj")[m % 2]
                self.dma_step("sync", sem, f"wo{m}",
                              lambda e, m=m: e.dma_start(
                                  out=self.wstg[m % 4][:, :],
                                  in_=self.wout_h[m * 128:(m + 1) * 128, :]))
            return go

        def mk_cast_wo(m):
            def go():
                self.wait("vector", f"wo{m}")

                def v_c(e, m=m):
                    e.drain()
                    return e.tensor_copy(self.woutT[:, m * C:(m + 1) * C],
                                         self.wstg[m % 4][:, :])
                self.step("vector", "v", f"wocast{m}", v_c)
            return go

        def mk_load_wf(m):
            def go():
                self.wait("gpsimd", f"wocast{4 + m}" if m < 4 else f"wfcast{m - 4}")
                sem = ("wg", "wi")[m % 2]
                self.dma_step("gpsimd", sem, f"wf{m}",
                              lambda e, m=m: e.dma_start(
                                  out=self.wstg[m % 4][:, :],
                                  in_=self.wffn_h[m * 128:(m + 1) * 128, :]))
            return go

        def mk_cast_wf(m):
            def go():
                self.wait("vector", f"wf{m}")

                def v_c(e, m=m):
                    e.drain()
                    e.tensor_copy(self.wf8[:], self.wstg[m % 4][:, :])
                    e.drain()
                    return e.tensor_copy(self.wffnT[:, m * C:(m + 1) * C], self.wf8[:])
                self.step("vector", "v", f"wfcast{m}", v_c)
            return go

        p = [mk_load_wo(0), mk_load_wo(1)]
        for m in range(2, 8):
            p += [mk_cast_wo(m - 2), mk_load_wo(m)]
        p += [mk_cast_wo(6), mk_cast_wo(7), mk_load_wf(0), mk_load_wf(1)]
        for m in range(2, 8):
            p += [mk_cast_wf(m - 2), mk_load_wf(m)]
        p += [mk_cast_wf(6), mk_cast_wf(7)]
        self.pending = p

    def _pump(self, n):
        while n and self.pending:
            self.pending.pop(0)()
            n -= 1

    def _emit_debug_taps(self):
        tensors = {
            "ln1_bf": self.ln1_bf, "ln1T": self.ln1T,
            "qraw0": self.qraw[0], "qraw1": self.qraw[1],
            "qraw2": self.qraw[2], "qraw3": self.qraw[3],
            "qrx0": self.qrx[0], "qrx1": self.qrx[1],
            "krx0": self.krx[0], "krx1": self.krx[1],
            "vaug": self.vaug, "expT": self.expT,
            "attn_out": self.attn_out, "attn_outT": self.attn_outT,
            "proj_b": self.proj_b, "t_pa": self.t_pa, "x2_sb": self.x2_sb,
            "ln2_f32": self.ln2_f8, "ln2T_f8": self.ln2T, "ffn_b": self.ffn_b,
            "t_ffn": self.t_ffn,
        }
        for name in ("t", "v", "s", "g"):
            self.emit("gpsimd",
                      lambda e, sn=name, v=self.counts[name]: e.wait_ge(self.sems[sn], v))
        for name, t in tensors.items():
            h = self.nc.dram_tensor("dbg_" + name, list(t.shape), t.dtype, kind="ExternalOutput")
            self.dma_step("gpsimd", "st", None,
                          lambda e, hh=h, tt=t: e.dma_start(out=hh[:, :], in_=tt[:]))

    def _emit_ln(self, gi, phase, src_sb, dst_sb, s1_pre=None):
        step, wait = self.step, self.wait
        base = (phase * 2 + gi % 2) * 8
        L = self.lnscr
        s1 = L[:, base + 0: base + 1]
        mean = L[:, base + 1: base + 2]
        msq = L[:, base + 2: base + 3]
        var = L[:, base + 3: base + 4]
        rin = L[:, base + 4: base + 5]
        rstd = L[:, base + 5: base + 6]
        nm = L[:, base + 6: base + 7]
        s2 = L[:, base + 7: base + 8]
        tag = f"ln{phase}_{gi}"

        def s_sq(e):
            return e.activation(self.sq_scr[:], src_sb[:], AF.Square, accum_out=s2)
        step("scalar", "s", f"{tag}_sq", s_sq)

        if s1_pre is None:
            def s_s1(e):
                e.drain()
                return e.activation(self.sq_scr[:], src_sb[:], AF.Copy, accum_out=s1)
            step("scalar", "s", f"{tag}_s1", s_s1)
            wait("vector", f"{tag}_s1")
        else:
            wait("vector", s1_pre)
        wait("vector", f"{tag}_sq")

        def v_stats(e):
            e.tensor_scalar(msq, s1, s1, 1.0 / (C * C), ALU.mult, ALU.mult)
            e.tensor_scalar(var, s2, 1.0 / C, EPS, ALU.mult, ALU.add)
            e.drain()
            return e.tensor_scalar(var, var, msq, None, ALU.subtract)
        step("vector", "v", f"{tag}_rin", v_stats)

        wait("scalar", f"{tag}_rin")

        def s_rstd(e):
            e.activation(rin, var, AF.Ln)
            e.drain()
            return e.activation(rstd, rin, AF.Exp, scale=-0.5)
        step("scalar", "s", f"{tag}_rstd", s_rstd)

        wait("vector", f"{tag}_rstd")

        def v_norm(e):
            e.tensor_scalar(nm, s1, rstd, -1.0 / C, ALU.mult, ALU.mult)
            e.drain()
            return e.tensor_scalar(dst_sb[:], src_sb[:], rstd, nm, ALU.mult, ALU.add)
        step("vector", "v", f"{tag}_done", v_norm)
        return f"{tag}_done"

    def _ctx(self, gi):
        b, m = divmod(gi, self.nt)
        kcs = [j for j in (m - 2, m - 1, m) if j >= 0]
        return {
            "gi": gi, "b": b, "m": m, "kcs": kcs, "nkc": len(kcs),
            "jr0": 3 - len(kcs),
            "slots": [(b * self.nt + j) % RING for j in kcs],
            "vslot": (b * self.nt + m) % RING,
            "xbuf": self.x_in[gi % 2],
            "obuf": self.x3_out[gi % 2],
        }

    def _emit_ln1(self, c):
        gi = c["gi"]
        self.wait("vector", f"x_tile{gi}")
        self.wait("scalar", f"x_tile{gi}")
        return self._emit_ln(gi, 0, c["xbuf"], self.ln1_bf)

    def _emit_ln1T(self, c, ln1_done):
        gi = c["gi"]
        self.wait("sync", ln1_done)

        def d_ln1T(e):
            return e.dma_start(out=self.ln1T[:].rearrange("p (c t) -> p c t", c=8),
                               in_=self.ln1_bf[:].rearrange("p (c t) -> p c t", c=8),
                               transpose=True)
        self.dma_step("sync", "tx", f"ln1T_{gi}", d_ln1T)

    def _emit_qkv(self, c):
        gi = c["gi"]
        step, wait, bank = self.step, self.wait, self.bank
        vslot = c["vslot"]
        wait("tensor", f"ln1T_{gi}")
        if gi == 0:
            wait("tensor", "wcast7")

        def mk_qk(cgs, mile):
            def t_qk(e, cgs=cgs):
                for cg in cgs:
                    bnk, off = cg // 4, (cg % 4) * 128
                    for kc in range(8):
                        ins = e.matmul(bank(bnk, off, off + 128),
                                       self.wqkvT[:, kc * 3 * C + cg * 128: kc * 3 * C + (cg + 1) * 128],
                                       self.ln1T[:, kc * 128:(kc + 1) * 128],
                                       start=(kc == 0), stop=(kc == 7))
                return ins
            step("tensor", "t", mile, t_qk)
        if gi == 0:
            mk_qk(range(8), "qkq_0")
            wait("tensor", "wcast15")
            mk_qk(range(8, 16), f"qk_done_{gi}")
            wait("tensor", "wcast23")
        else:
            mk_qk(range(16), f"qk_done_{gi}")

        def t_v(e):
            for nb in range(2):
                for kc in range(8):
                    ins = e.matmul(bank(4 + nb),
                                   self.ln1T[:, kc * 128:(kc + 1) * 128],
                                   self.wqkvT[:, kc * 3 * C + 2048 + nb * 512: kc * 3 * C + 2048 + (nb + 1) * 512],
                                   start=(kc == 0), stop=(kc == 7))
            return ins
        step("tensor", "t", f"v_done_{gi}", t_v)

        wait("scalar", f"qk_done_{gi}")

        def s_qk_evac(e):
            for i in range(4):
                ins = e.activation(self.qraw[i][:], bank(i), AF.Copy, scale=(0.125 if i < 2 else 1.0))
            return ins
        step("scalar", "s", f"qk_evac_{gi}", s_qk_evac)

        wait("scalar", f"v_done_{gi}")

        def s_v_evac(e):
            dstv = self.vaug[:].rearrange("p (r h c) -> p r h c", r=RING, h=16)
            for nb in range(2):
                ins = e.activation(dstv[:, vslot, nb * 8:(nb + 1) * 8, 0:64], bank(4 + nb), AF.Copy)
            return ins
        step("scalar", "s", f"v_evac_{gi}", s_v_evac)

    def _emit_rope(self, c):
        gi, m, vslot = c["gi"], c["m"], c["vslot"]
        step, wait = self.step, self.wait
        wait("vector", f"qk_evac_{gi}")
        if gi == 0:
            wait("vector", "tables")
        tcos = self.cos_sb[:, m * 128:(m + 1) * 128].unsqueeze(1).broadcast_to([128, 4, 128])
        tsin = self.sin_sb[:, m * 128:(m + 1) * 128].unsqueeze(1).broadcast_to([128, 4, 128])

        def v_rope(e):
            g4 = lambda ap: ap.rearrange("p (g t) -> p g t", g=4)
            for qk in range(2):
                x1 = g4(self.qraw[2 * qk][:])
                x2 = g4(self.qraw[2 * qk + 1][:])
                if qk == 0:
                    o1, o2 = g4(self.qrx[0][:]), g4(self.qrx[1][:])
                else:
                    o1 = g4(self.krx[0][:, vslot * 512:(vslot + 1) * 512])
                    o2 = g4(self.krx[1][:, vslot * 512:(vslot + 1) * 512])
                t1, t2 = g4(self.rp_t[0][:]), g4(self.rp_t[1][:])
                e.drain()
                e.tensor_tensor(t1, x1, tcos, ALU.mult)
                e.tensor_tensor(t2, x2, tsin, ALU.mult)
                e.drain()
                e.tensor_tensor(o1, t1, t2, ALU.subtract)
                e.drain()
                e.tensor_tensor(t1, x1, tsin, ALU.mult)
                e.tensor_tensor(t2, x2, tcos, ALU.mult)
                e.drain()
                ins = e.tensor_tensor(o2, t1, t2, ALU.add)
            return ins
        step("vector", "v", f"rope_{gi}", v_rope)

    SC_BANKS = [[0, 1, 2, 3], [4, 5, 6, 7], [5, 1, 2, 3], [7, 1, 2, 3]]
    AV_BANKS = [0, 4, 6, 5]

    def _emit_attention(self, c, nxt=None):
        gi, nkc, jr0, slots = c["gi"], c["nkc"], c["jr0"], c["slots"]
        step, wait, bank = self.step, self.wait, self.bank
        self.ln1_done_nxt = None
        expT_v = self.expT[:].rearrange("p (j h t) -> p j h t", j=3, h=H)
        vaug_v = self.vaug[:].rearrange("p (r hc) -> p r hc", r=RING)

        def emit_scores_head(w, i):
            h = 4 * w + i
            if w == 0:
                if h == 0:
                    wait("tensor", f"rope_{gi}")
                    wait("tensor", f"v_evac_{gi}")
                    if gi == 0:
                        wait("tensor", "v_prologue")
                        wait("tensor", "tables")
                if gi > 0:
                    wait("tensor", f"exphead_{gi - 1}_{12 + i}")
            else:
                wait("tensor", f"exphead_{gi}_{h - 4}")

            def t_sc(e, h=h, i=i):
                strip = i * 32
                g = h // 4
                e.matmul(bank(i, 0, nkc * 128), self.idbf[:],
                         self.mask3[:, jr0 * 128:384], start=True, stop=False)
                for jr in range(nkc):
                    sl = slots[jr]
                    for par in range(2):
                        ins = e.matmul(bank(i, jr * 128, (jr + 1) * 128),
                                       self.krx[par][strip:strip + 32,
                                                     sl * 512 + g * 128: sl * 512 + (g + 1) * 128],
                                       self.qrx[par][strip:strip + 32, g * 128:(g + 1) * 128],
                                       start=False, stop=(par == 1 and jr == nkc - 1),
                                       tile_position=(strip, 0))
                return ins
            step("tensor", "t", f"schead_{gi}_{h}", t_sc)

            wait("scalar", f"schead_{gi}_{h}")

            def s_exp(e, h=h, i=i):
                return e.activation(expT_v[:, jr0:3, h, :], bank(i, 0, nkc * 128), AF.Exp)
            step("scalar", "s", f"exphead_{gi}_{h}", s_exp)

        def emit_av(w):
            heads = tuple(range(4 * w, 4 * w + 4))
            if w >= 2:
                wait("tensor", f"aoev_{gi}_w{w - 2}")
            for h in range(4 * w, 4 * w + 4):
                wait("tensor", f"exphead_{gi}_{h}")

                def t_av(e, w=w, h=h):
                    ab = 4 + (w % 2)
                    off = (h % 4) * 128
                    for jr in range(nkc):
                        ins = e.matmul(bank(ab, off, off + 65),
                                       expT_v[:, jr0 + jr, h, :],
                                       vaug_v[:, slots[jr], h * 65:(h + 1) * 65],
                                       start=(jr == 0), stop=(jr == nkc - 1))
                    return ins
                step("tensor", "t",
                     f"av_{gi}_w{w}" if h == 4 * w + 3 else None, t_av)

            wait("vector", f"av_{gi}_w{w}")

            def v_div(e, w=w, heads=heads):
                ab = 4 + (w % 2)
                for h in heads:
                    off = (h % 4) * 128
                    e.reciprocal(self.rd_sb[:, h:h + 1], bank(ab, off + 64, off + 65))
                e.drain()
                for h in heads:
                    off = (h % 4) * 128
                    ins = e.tensor_scalar(self.attn_out[:, h * 64:(h + 1) * 64],
                                          bank(ab, off, off + 64),
                                          self.rd_sb[:, h:h + 1], None, ALU.mult)
                return ins
            step("vector", "v", f"div_{gi}_w{w}", v_div)
            if gi == 0:
                self._pump(4)

        def emit_trans_proj(w):
            wait("tensor", f"div_{gi}_w{w}")
            abbf = bank(4 + (w % 2)).bitcast(BF16)

            def t_tr(e, w=w):
                for k in range(2):
                    cc = 2 * w + k
                    ins = e.transpose(abbf[:, k * 128:(k + 1) * 128],
                                      self.attn_out[:, cc * 128:(cc + 1) * 128], self.idbf[:])
                return ins
            step("tensor", "t", f"aotr_{gi}_w{w}", t_tr)

            wait("scalar", f"aotr_{gi}_w{w}")

            def s_ev(e, w=w):
                return e.activation(self.attn_outT[:, w * 256:(w + 1) * 256],
                                    abbf[:, 0:256], AF.Copy)
            step("scalar", "s", f"aoev_{gi}_w{w}", s_ev)

            wait("tensor", f"aoev_{gi}_w{w}")
            if gi == 0:
                wait("tensor", f"wocast{2 * w + 1}")

            def t_proj(e, w=w):
                for nb in range(2):
                    for k in range(2):
                        cc = 2 * w + k
                        ins = e.matmul(bank(6 + nb),
                                       self.attn_outT[:, cc * 128:(cc + 1) * 128],
                                       self.woutT[:, cc * C + nb * 512: cc * C + (nb + 1) * 512],
                                       start=(w == 0 and k == 0), stop=(w == 3 and k == 1))
                return ins
            mile = f"proj_{gi}" if w == 3 else None
            step("tensor", "t", mile or f"projw_{gi}_w{w}", t_proj)

        for i in range(4):
            emit_scores_head(0, i)
        if self.pending_ffn is not None:
            pf, self.pending_ffn = self.pending_ffn, None
            pf()
        for i in range(4):
            emit_scores_head(1, i)
        emit_av(0)
        if nxt is not None:
            self.ln1_done_nxt = self._emit_ln1(nxt)
        for i in range(4):
            emit_scores_head(2, i)
        emit_av(1)
        emit_trans_proj(0)
        for i in range(4):
            emit_scores_head(3, i)
        emit_av(2)
        emit_trans_proj(1)
        emit_av(3)
        emit_trans_proj(2)
        emit_trans_proj(3)

    def _emit_tail(self, c, nxt=None):
        gi, xbuf, obuf = c["gi"], c["xbuf"], c["obuf"]
        step, wait, emit, bank = self.step, self.wait, self.emit, self.bank

        wait("vector", f"proj_{gi}")
        s1_pa = self.lnscr[:, (1 * 2 + gi % 2) * 8: (1 * 2 + gi % 2) * 8 + 1]

        if gi == 0:
            wait("vector", "boutld")

        def v_proj_b(e):
            return e.scalar_tensor_tensor(self.proj_b[:], self.ps_all[:, 6 * 512: 8 * 512], 1.0,
                                          self.boutB[:], ALU.mult, ALU.add, accum_out=s1_pa)
        step("vector", "v", f"proj_b_{gi}", v_proj_b)
        if gi == 0:
            self._pump(6)

        if nxt:
            self._emit_ln1T(nxt, self.ln1_done_nxt)

        wait("scalar", f"proj_b_{gi}")
        tpa_done = self._emit_ln(gi, 1, self.proj_b, self.t_pa, s1_pre=f"proj_b_{gi}")

        s1_x2 = self.lnscr[:, (2 * 2 + gi % 2) * 8: (2 * 2 + gi % 2) * 8 + 1]
        wait("vector", tpa_done)
        if gi > 0:
            wait("vector", f"x3_{gi - 1}")

        def v_x2(e):
            return e.scalar_tensor_tensor(self.x2_sb[:], xbuf[:], 1.0, self.t_pa[:],
                                          ALU.mult, ALU.add, accum_out=s1_x2)
        step("vector", "v", f"x2_{gi}", v_x2)
        if gi == 0:
            self._pump(99)

        if gi + 2 < self.n_tiles:
            wait("sync", f"x2_{gi}")
            self.dma_step("sync", "x", f"x_tile{gi + 2}",
                          lambda e, g=gi + 2: e.dma_start(out=self.x_in[g % 2][:],
                                                          in_=self.x_h[g * 128:(g + 1) * 128, :]))

        wait("scalar", f"x2_{gi}")
        ln2_done = self._emit_ln(gi, 2, self.x2_sb, self.ln2_f8, s1_pre=f"x2_{gi}")

        def v_ln2bf(e):
            e.drain()
            return e.tensor_copy(self.ln2_bf[:], self.ln2_f8[:])
        step("vector", "v", f"ln2bf_{gi}", v_ln2bf)

        if nxt:
            self._emit_qkv(nxt)
            self._emit_rope(nxt)

        wait("sync", f"ln2bf_{gi}")

        def d_ln2T_a(e):
            return e.dma_start(out=self.ln2T[:, 0:512].rearrange("p (c t) -> p c t", c=4),
                               in_=self.ln2_bf[:, 0:512].rearrange("p (c t) -> p c t", c=4),
                               transpose=True)
        self.dma_step("sync", "tx", f"ln2T_{gi}a", d_ln2T_a)

        def d_ln2T_b(e):
            return e.dma_start(out=self.ln2T[:, 512:1024].rearrange("p (c t) -> p c t", c=4),
                               in_=self.ln2_bf[:, 512:1024].rearrange("p (c t) -> p c t", c=4),
                               transpose=True)
        self.dma_step("sync", "tx", f"ln2T_{gi}b", d_ln2T_b)

        def emit_ffn_tail(c_=c):
            gi, obuf = c_["gi"], c_["obuf"]

            def mm_ffn(e, cch, nb):
                return e.matmul(bank(6 + nb),
                                self.ln2T[:, cch * 128:(cch + 1) * 128],
                                self.wffnT[:, cch * C + nb * 512: cch * C + (nb + 1) * 512],
                                start=(cch == 0), stop=(cch == 7))

            wait("tensor", f"ln2T_{gi}a")
            wait("tensor", f"proj_b_{gi}")
            if gi == 0:
                wait("tensor", "wfcast7")

            def t_ffn_a(e):
                for cch in range(4):
                    for nb in range(2):
                        ins = mm_ffn(e, cch, nb)
                return ins
            step("tensor", "t", None, t_ffn_a)
            wait("tensor", f"ln2T_{gi}b")

            def t_ffn_b(e):
                for cch in range(4, 8):
                    for nb in range(2):
                        ins = mm_ffn(e, cch, nb)
                return ins
            step("tensor", "t", f"ffn_{gi}", t_ffn_b)

            wait("vector", f"ffn_{gi}")
            s1_f = self.lnscr[:, (3 * 2 + gi % 2) * 8: (3 * 2 + gi % 2) * 8 + 1]

            if gi == 0:
                wait("vector", "bffnld")

            def v_ffn_b(e):
                return e.scalar_tensor_tensor(self.ffn_b[:], self.ps_all[:, 6 * 512: 8 * 512], 1.0,
                                              self.bffnB[:], ALU.mult, ALU.add, accum_out=s1_f)
            step("vector", "v", f"ffn_b_{gi}", v_ffn_b)

            wait("scalar", f"ffn_b_{gi}")
            if gi > 0:
                wait("scalar", f"x3_{gi - 1}")
            tf_done = self._emit_ln(gi, 3, self.ffn_b, self.t_ffn, s1_pre=f"ffn_b_{gi}")

            wait("gpsimd", tf_done)
            if gi >= 2:
                emit("gpsimd", lambda e, v=16 * (gi - 1): e.wait_ge(self.sems["st"], v))

            def g_x3(e):
                return e.tensor_tensor(obuf[:], self.x2_sb[:], self.t_ffn[:], ALU.add)
            step("gpsimd", "g", f"x3_{gi}", g_x3)
            wait("gpsimd", f"x3_{gi}")

            def g_store(e):
                return e.dma_start(out=self.out_h[gi * 128:(gi + 1) * 128, :], in_=obuf[:])
            self.dma_step("gpsimd", "st", f"store_{gi}", g_store)

        if nxt:
            self.pending_ffn = emit_ffn_tail
        else:
            emit_ffn_tail()


def build_program(b_core=B_CORE, nt=NT, debug=False):
    return _Builder(b_core, nt, debug=debug).build()


DBG_TAPS = ("ln1_bf", "ln1T", "qraw0", "qraw1", "qraw2", "qraw3", "qrx0", "qrx1",
            "krx0", "krx1", "vaug", "expT", "attn_out", "attn_outT", "proj_b",
            "t_pa", "x2_sb", "ln2_f32", "ln2T_f8", "ffn_b", "t_ffn")


def host_inputs(w_qkv, w_out, b_out, w_ffn, b_ffn, t_seq=T_SEQ):
    perm = _qkv_row_perm()
    wqkvT = np.ascontiguousarray(w_qkv[perm].T)
    woutT = np.ascontiguousarray(w_out.T)
    wffnT = np.ascontiguousarray(w_ffn.T)
    boutB = np.ascontiguousarray(np.broadcast_to(b_out[None, :], (128, C)))
    bffnB = np.ascontiguousarray(np.broadcast_to(b_ffn[None, :], (128, C)))
    cos_t, sin_t = _rope_tables(t_seq)
    kk = np.arange(128)[:, None]
    qq = np.arange(128)[None, :]
    return {
        "wqkvT": wqkvT, "woutT": woutT, "wffnT": wffnT,
        "boutB": boutB, "bffnB": bffnB,
        "ropecos": cos_t, "ropesin": sin_t,
        "mask3": np.concatenate([
            np.where(kk >= qq, 0.0, -1e5),
            np.zeros((128, 128)),
            np.where(kk <= qq, 0.0, -1e5)], axis=1).astype(ml_dtypes.bfloat16),
        "identbf": np.eye(128, dtype=ml_dtypes.bfloat16),
    }


def kernel(L0_attn_x, L0_attn_w_qkv, L0_attn_w_out, L0_attn_b_out,
           L0_ffn_weight, L0_ffn_bias):
    x = np.ascontiguousarray(np.asarray(L0_attn_x, dtype=np.float32))
    shared = host_inputs(np.asarray(L0_attn_w_qkv, dtype=np.float32),
                         np.asarray(L0_attn_w_out, dtype=np.float32),
                         np.asarray(L0_attn_b_out, dtype=np.float32),
                         np.asarray(L0_ffn_weight, dtype=np.float32),
                         np.asarray(L0_ffn_bias, dtype=np.float32))
    nc = build_program()
    in_maps = []
    for c in range(N_CORES):
        im = dict(shared)
        im["x"] = np.ascontiguousarray(
            x[c * B_CORE:(c + 1) * B_CORE].reshape(B_CORE * T_SEQ, C))
        in_maps.append(im)
    res = run_bass_kernel_spmd(nc, in_maps, core_ids=list(range(N_CORES)))
    out = np.concatenate([r["out"].reshape(B_CORE, T_SEQ, C) for r in res.results], axis=0)
    return out.astype(np.float32)


if __name__ == "__main__":
    build_program()
    print("built ok")


# version 39
# speedup vs baseline: 1.0883x; 1.0883x over previous
import numpy as np
import ml_dtypes

import concourse.bass as bass
import concourse.mybir as mybir
from concourse.bass_utils import run_bass_kernel_spmd

F32 = mybir.dt.float32
BF16 = mybir.dt.bfloat16
F8 = mybir.dt.float8e4
AF = mybir.ActivationFunctionType
ALU = mybir.AluOpType

N_CORES = 8
B_FULL = 16
T_SEQ = 2048
C = 1024
H = 16
D = 64
QT = 128
EPS = 1e-5
RING = 4

B_CORE = B_FULL // N_CORES
NT = T_SEQ // QT


def _qkv_row_perm():
    rows = []
    for base, par in ((0, 0), (0, 1), (64, 0), (64, 1)):
        for h in range(H):
            for j in range(32):
                rows.append(h * 192 + base + 2 * j + par)
    for h in range(H):
        for d in range(D):
            rows.append(h * 192 + 128 + d)
    assert len(rows) == 3 * C and len(set(rows)) == 3 * C
    return np.asarray(rows, dtype=np.int64)


def _rope_tables(t_seq):
    d2 = D // 2
    theta = (10000.0 ** (-(np.arange(d2, dtype=np.float32) / np.float32(d2)))).astype(np.float32)
    ang32 = np.arange(t_seq, dtype=np.float32)[:, None] * theta[None, :]
    cosv = np.cos(ang32.astype(np.float64)).astype(np.float32)
    sinv = np.sin(ang32.astype(np.float64)).astype(np.float32)
    p = np.arange(128) % d2
    return (np.ascontiguousarray(cosv[:, p].T).astype(ml_dtypes.bfloat16),
            np.ascontiguousarray(sinv[:, p].T).astype(ml_dtypes.bfloat16))


class _Builder:
    def __init__(self, b_core, nt, debug=False):
        self.debug = debug
        self.b_core, self.nt = b_core, nt
        self.t_seq = nt * QT
        self.n_tiles = b_core * nt
        self.queues = {k: [] for k in ("sync", "gpsimd", "vector", "scalar", "tensor")}
        self.counts = {k: 0 for k in ("x", "tx", "ty", "wg", "wh", "wi", "wj", "wo", "wf", "st", "t", "v", "s", "g")}
        self.miles = {}
        self.sems = {}

    def emit(self, eng, fn):
        self.queues[eng].append(fn)

    def wait(self, eng, milestone):
        sem_name, val = self.miles[milestone]
        self.emit(eng, lambda e, sn=sem_name, v=val: e.wait_ge(self.sems[sn], v))

    def step(self, eng, sem_name, milestone, fn, by=1):
        def run(e, f=fn, sn=sem_name, b=by):
            ins = f(e)
            ins.then_inc(self.sems[sn], b)
        self.emit(eng, run)
        self.counts[sem_name] += by
        if milestone is not None:
            self.miles[milestone] = (sem_name, self.counts[sem_name])

    def dma_step(self, eng, sem_name, milestone, fn):
        prev = self.counts[sem_name]
        if prev:
            self.emit(eng, lambda e, sn=sem_name, v=prev: e.wait_ge(self.sems[sn], v))
        self.step(eng, sem_name, milestone, fn, by=16)

    def build(self):
        nc = bass.Bass("TRN2", target_bir_lowering=False)
        self.nc = nc
        t_seq, nt = self.t_seq, self.nt

        self.x_h = nc.dram_tensor("x", [self.b_core * t_seq, C], F32, kind="ExternalInput")
        self.wqkv_h = nc.dram_tensor("wqkvT", [C, 3 * C], F32, kind="ExternalInput")
        self.wout_h = nc.dram_tensor("woutT", [C, C], F32, kind="ExternalInput")
        self.wffn_h = nc.dram_tensor("wffnT", [C, C], F32, kind="ExternalInput")
        self.bout_h = nc.dram_tensor("boutB", [128, C], F32, kind="ExternalInput")
        self.bffn_h = nc.dram_tensor("bffnB", [128, C], F32, kind="ExternalInput")
        self.cos_h = nc.dram_tensor("ropecos", [128, t_seq], BF16, kind="ExternalInput")
        self.sin_h = nc.dram_tensor("ropesin", [128, t_seq], BF16, kind="ExternalInput")
        self.mask3_h = nc.dram_tensor("mask3", [128, 384], BF16, kind="ExternalInput")
        self.idbf_h = nc.dram_tensor("identbf", [128, 128], BF16, kind="ExternalInput")
        self.out_h = nc.dram_tensor("out", [self.b_core * t_seq, C], F32, kind="ExternalOutput")

        from contextlib import ExitStack
        es = ExitStack()
        sbuf = lambda n, sh, dt: es.enter_context(nc.sbuf_tensor(n, sh, dt))

        self.wqkvT = sbuf("wqkvT_sb", [128, 8 * 3 * C], BF16)
        self.woutT = sbuf("woutT_sb", [128, 8 * C], BF16)
        self.wffnT = sbuf("wffnT_sb", [128, 8 * C], BF16)
        self.wf8 = sbuf("wf8_sb", [128, C], F8)
        self.wstg = [sbuf(f"wstg{i}", [128, C], F32) for i in range(4)]
        self.cos_sb = sbuf("cos_sb", [128, t_seq], BF16)
        self.sin_sb = sbuf("sin_sb", [128, t_seq], BF16)
        self.boutB = sbuf("boutB_sb", [128, C], F32)
        self.bffnB = sbuf("bffnB_sb", [128, C], F32)
        self.mask3 = sbuf("mask3_sb", [128, 384], BF16)
        self.idbf = sbuf("idbf_sb", [128, 128], BF16)
        self.x_in = [sbuf(f"x_in{i}", [128, C], F32) for i in range(2)]
        self.x3_out = [sbuf(f"x3_out{i}", [128, C], F32) for i in range(2)]
        self.ln1_bf = sbuf("ln1_bf", [128, C], BF16)
        self.ln1T = sbuf("ln1T", [128, C], BF16)
        self.qraw = [sbuf(f"qraw{i}", [128, 512], BF16) for i in range(4)]
        self.rp_t = [sbuf(f"rp_t{i}", [128, 512], BF16) for i in range(2)]
        self.qrx = [sbuf(f"qrx{i}", [128, 512], BF16) for i in range(2)]
        self.krx = [sbuf(f"krx{i}", [128, RING * 512], BF16) for i in range(2)]
        self.vaug = sbuf("vaug", [128, RING * 16 * 65], BF16)
        self.expT = sbuf("expT", [128, 3 * H * 128], BF16)
        self.attn_out = sbuf("attn_out", [128, C], BF16)
        self.attn_outT = sbuf("attn_outT", [128, C], BF16)
        self.rd_sb = sbuf("rd_sb", [128, H], F32)
        self.proj_b = sbuf("proj_b", [128, C], F32)
        self.t_pa = sbuf("t_pa", [128, C], F32)
        self.x2_sb = sbuf("x2_sb", [128, C], F32)
        self.ln2_f8 = sbuf("ln2_f8", [128, C], F8)
        self.ln2_bf = sbuf("ln2_bf", [128, C], BF16)
        self.ln2T = sbuf("ln2T_bf", [128, C], BF16)
        self.ffn_b = sbuf("ffn_b", [128, C], F32)
        self.t_ffn = sbuf("t_ffn", [128, C], F32)
        self.sq_scr = sbuf("sq_scr", [128, C], BF16)
        self.lnscr = sbuf("lnscr", [128, 4 * 2 * 8], F32)
        self.dbg_expin = sbuf("dbg_expin", [128, 3 * 128], F32)

        self.ps_all = es.enter_context(nc.psum_tensor("ps_all", [128, 4096], F32))
        for n in ("x", "tx", "ty", "wg", "wh", "wi", "wj", "wo", "wf", "st", "t", "v", "s", "g"):
            self.sems[n] = es.enter_context(nc.semaphore(f"sem_{n}"))

        c0 = self._ctx(0)
        self._prologue(c0)
        self._emit_qkv(c0)
        self._emit_rope(c0)
        self._pump(4)
        for gi in range(self.n_tiles):
            c = self._ctx(gi)
            nxt = self._ctx(gi + 1) if gi + 1 < self.n_tiles else None
            self._emit_attention(c, nxt)
            self._emit_tail(c, nxt)
        if self.debug:
            self._emit_debug_taps()
        total_st = self.counts["st"]
        self.emit("gpsimd", lambda e, v=total_st: e.wait_ge(self.sems["st"], v))

        with es, nc.Block() as block:
            @block.sync
            def _(eng):
                for fn in self.queues["sync"]:
                    fn(eng)

            @block.gpsimd
            def _(eng):
                for fn in self.queues["gpsimd"]:
                    fn(eng)

            @block.vector
            def _(eng):
                for fn in self.queues["vector"]:
                    fn(eng)

            @block.scalar
            def _(eng):
                for fn in self.queues["scalar"]:
                    fn(eng)

            @block.tensor
            def _(eng):
                for fn in self.queues["tensor"]:
                    fn(eng)
        return nc

    def bank(self, b, lo=0, hi=512):
        return self.ps_all[:, b * 512 + lo: b * 512 + hi]

    def _prologue(self, c0):
        step, wait = self.step, self.wait

        def ld(eng, sem, dst, src, mile=None):
            self.dma_step(eng, sem, mile,
                 lambda e, d=dst, s=src: e.dma_start(out=d[:], in_=s[:, :]))
        ld("sync", "x", self.x_in[0], self.x_h[0:128], "x_tile0")
        if self.n_tiles > 1:
            ld("sync", "x", self.x_in[1], self.x_h[128:256], "x_tile1")
        ln1_done0 = self._emit_ln1(c0)
        self._emit_ln1T(c0, ln1_done0)
        for dst, src in ((self.cos_sb, self.cos_h), (self.sin_sb, self.sin_h),
                         (self.idbf, self.idbf_h), (self.mask3, self.mask3_h)):
            ld("scalar", "wh", dst, src)
        self.miles["tables"] = ("wh", self.counts["wh"])

        jobs = []
        for hf in range(3):
            for cch in range(8):
                src = self.wqkv_h[cch * 128:(cch + 1) * 128, hf * C:(hf + 1) * C]
                jobs.append(("gpsimd", f"wqkv{cch}_{hf}", src, cch * 3 * C + hf * C))
        qsems = {"gpsimd": ("wg", "wi"), "sync": ("wh", "wj")}
        qn = {"gpsimd": 0, "sync": 0}
        for idx, (q, name, src, off) in enumerate(jobs):
            if idx >= 4:
                wait(q, f"wcast{idx - 4}")
            sem = qsems[q][qn[q] % 2]
            qn[q] += 1
            self.dma_step(q, sem, name,
                          lambda e, s_=src, b_=idx % 4: e.dma_start(
                              out=self.wstg[b_][:, :], in_=s_))
            wait("vector", name)

            def v_cast(e, o_=off, b_=idx % 4):
                e.drain()
                return e.tensor_copy(self.wqkvT[:, o_:o_ + C], self.wstg[b_][:, :])
            step("vector", "v", f"wcast{idx}", v_cast)
        sem = qsems["gpsimd"][qn["gpsimd"] % 2]
        self.dma_step("gpsimd", sem, "boutld",
                      lambda e: e.dma_start(out=self.boutB[:], in_=self.bout_h[:, :]))
        sem = qsems["gpsimd"][(qn["gpsimd"] + 1) % 2]
        self.dma_step("gpsimd", sem, "bffnld",
                      lambda e: e.dma_start(out=self.bffnB[:], in_=self.bffn_h[:, :]))

        def v_prologue(e):
            ones = self.vaug[:].rearrange("p (r h c) -> p r h c", r=RING, h=16)
            return e.memset(ones[:, :, :, 64:65], 1.0)
        step("vector", "v", "v_prologue", v_prologue)

        def mk_load_wo(m):
            def go():
                self.wait("sync", f"wcast{20 + m}" if m < 4 else f"wocast{m - 4}")
                sem = ("wh", "wj")[m % 2]
                self.dma_step("sync", sem, f"wo{m}",
                              lambda e, m=m: e.dma_start(
                                  out=self.wstg[m % 4][:, :],
                                  in_=self.wout_h[m * 128:(m + 1) * 128, :]))
            return go

        def mk_cast_wo(m):
            def go():
                self.wait("vector", f"wo{m}")

                def v_c(e, m=m):
                    e.drain()
                    return e.tensor_copy(self.woutT[:, m * C:(m + 1) * C],
                                         self.wstg[m % 4][:, :])
                self.step("vector", "v", f"wocast{m}", v_c)
            return go

        def mk_load_wf(m):
            def go():
                self.wait("gpsimd", f"wocast{4 + m}" if m < 4 else f"wfcast{m - 4}")
                sem = ("wg", "wi")[m % 2]
                self.dma_step("gpsimd", sem, f"wf{m}",
                              lambda e, m=m: e.dma_start(
                                  out=self.wstg[m % 4][:, :],
                                  in_=self.wffn_h[m * 128:(m + 1) * 128, :]))
            return go

        def mk_cast_wf(m):
            def go():
                self.wait("vector", f"wf{m}")

                def v_c(e, m=m):
                    e.drain()
                    e.tensor_copy(self.wf8[:], self.wstg[m % 4][:, :])
                    e.drain()
                    return e.tensor_copy(self.wffnT[:, m * C:(m + 1) * C], self.wf8[:])
                self.step("vector", "v", f"wfcast{m}", v_c)
            return go

        p = [mk_load_wo(0), mk_load_wo(1)]
        for m in range(2, 8):
            p += [mk_cast_wo(m - 2), mk_load_wo(m)]
        p += [mk_cast_wo(6), mk_cast_wo(7), mk_load_wf(0), mk_load_wf(1)]
        for m in range(2, 8):
            p += [mk_cast_wf(m - 2), mk_load_wf(m)]
        p += [mk_cast_wf(6), mk_cast_wf(7)]
        self.pending = p

    def _pump(self, n):
        while n and self.pending:
            self.pending.pop(0)()
            n -= 1

    def _emit_debug_taps(self):
        tensors = {
            "ln1_bf": self.ln1_bf, "ln1T": self.ln1T,
            "qraw0": self.qraw[0], "qraw1": self.qraw[1],
            "qraw2": self.qraw[2], "qraw3": self.qraw[3],
            "qrx0": self.qrx[0], "qrx1": self.qrx[1],
            "krx0": self.krx[0], "krx1": self.krx[1],
            "vaug": self.vaug, "expT": self.expT,
            "attn_out": self.attn_out, "attn_outT": self.attn_outT,
            "proj_b": self.proj_b, "t_pa": self.t_pa, "x2_sb": self.x2_sb,
            "ln2_f32": self.ln2_f8, "ln2T_f8": self.ln2T, "ffn_b": self.ffn_b,
            "t_ffn": self.t_ffn,
        }
        for name in ("t", "v", "s", "g"):
            self.emit("gpsimd",
                      lambda e, sn=name, v=self.counts[name]: e.wait_ge(self.sems[sn], v))
        for name, t in tensors.items():
            h = self.nc.dram_tensor("dbg_" + name, list(t.shape), t.dtype, kind="ExternalOutput")
            self.dma_step("gpsimd", "st", None,
                          lambda e, hh=h, tt=t: e.dma_start(out=hh[:, :], in_=tt[:]))

    def _emit_ln(self, gi, phase, src_sb, dst_sb, s1_pre=None):
        step, wait = self.step, self.wait
        base = (phase * 2 + gi % 2) * 8
        L = self.lnscr
        s1 = L[:, base + 0: base + 1]
        mean = L[:, base + 1: base + 2]
        msq = L[:, base + 2: base + 3]
        var = L[:, base + 3: base + 4]
        rin = L[:, base + 4: base + 5]
        rstd = L[:, base + 5: base + 6]
        nm = L[:, base + 6: base + 7]
        s2 = L[:, base + 7: base + 8]
        tag = f"ln{phase}_{gi}"

        def s_sq(e):
            return e.activation(self.sq_scr[:], src_sb[:], AF.Square, accum_out=s2)
        step("scalar", "s", f"{tag}_sq", s_sq)

        if s1_pre is None:
            def s_s1(e):
                e.drain()
                return e.activation(self.sq_scr[:], src_sb[:], AF.Copy, accum_out=s1)
            step("scalar", "s", f"{tag}_s1", s_s1)
            wait("vector", f"{tag}_s1")
        else:
            wait("vector", s1_pre)
        wait("vector", f"{tag}_sq")

        def v_stats(e):
            e.tensor_scalar(msq, s1, s1, 1.0 / (C * C), ALU.mult, ALU.mult)
            e.tensor_scalar(var, s2, 1.0 / C, EPS, ALU.mult, ALU.add)
            e.drain()
            return e.tensor_scalar(var, var, msq, None, ALU.subtract)
        step("vector", "v", f"{tag}_rin", v_stats)

        wait("scalar", f"{tag}_rin")

        def s_rstd(e):
            e.activation(rin, var, AF.Ln)
            e.drain()
            return e.activation(rstd, rin, AF.Exp, scale=-0.5)
        step("scalar", "s", f"{tag}_rstd", s_rstd)

        wait("vector", f"{tag}_rstd")

        def v_norm(e):
            e.tensor_scalar(nm, s1, rstd, -1.0 / C, ALU.mult, ALU.mult)
            e.drain()
            return e.tensor_scalar(dst_sb[:], src_sb[:], rstd, nm, ALU.mult, ALU.add)
        step("vector", "v", f"{tag}_done", v_norm)
        return f"{tag}_done"

    def _ctx(self, gi):
        b, m = divmod(gi, self.nt)
        kcs = [j for j in (m - 2, m - 1, m) if j >= 0]
        return {
            "gi": gi, "b": b, "m": m, "kcs": kcs, "nkc": len(kcs),
            "jr0": 3 - len(kcs),
            "slots": [(b * self.nt + j) % RING for j in kcs],
            "vslot": (b * self.nt + m) % RING,
            "xbuf": self.x_in[gi % 2],
            "obuf": self.x3_out[gi % 2],
        }

    def _emit_ln1(self, c):
        gi = c["gi"]
        self.wait("vector", f"x_tile{gi}")
        self.wait("scalar", f"x_tile{gi}")
        return self._emit_ln(gi, 0, c["xbuf"], self.ln1_bf)

    def _emit_ln1T(self, c, ln1_done):
        gi = c["gi"]
        self.wait("sync", ln1_done)

        def d_ln1T(e):
            return e.dma_start(out=self.ln1T[:].rearrange("p (c t) -> p c t", c=8),
                               in_=self.ln1_bf[:].rearrange("p (c t) -> p c t", c=8),
                               transpose=True)
        self.dma_step("sync", "tx", f"ln1T_{gi}", d_ln1T)

    def _emit_qkv(self, c):
        gi = c["gi"]
        step, wait, bank = self.step, self.wait, self.bank
        vslot = c["vslot"]
        wait("tensor", f"ln1T_{gi}")
        if gi == 0:
            wait("tensor", "wcast7")

        def mk_qk(cgs, mile):
            def t_qk(e, cgs=cgs):
                for cg in cgs:
                    bnk, off = cg // 4, (cg % 4) * 128
                    for kc in range(8):
                        ins = e.matmul(bank(bnk, off, off + 128),
                                       self.wqkvT[:, kc * 3 * C + cg * 128: kc * 3 * C + (cg + 1) * 128],
                                       self.ln1T[:, kc * 128:(kc + 1) * 128],
                                       start=(kc == 0), stop=(kc == 7))
                return ins
            step("tensor", "t", mile, t_qk)
        if gi == 0:
            mk_qk(range(8), "qkq_0")
            wait("tensor", "wcast15")
            mk_qk(range(8, 16), f"qk_done_{gi}")
            wait("tensor", "wcast23")
        else:
            mk_qk(range(16), f"qk_done_{gi}")

        def t_v(e):
            for nb in range(2):
                for kc in range(8):
                    ins = e.matmul(bank(4 + nb),
                                   self.ln1T[:, kc * 128:(kc + 1) * 128],
                                   self.wqkvT[:, kc * 3 * C + 2048 + nb * 512: kc * 3 * C + 2048 + (nb + 1) * 512],
                                   start=(kc == 0), stop=(kc == 7))
            return ins
        step("tensor", "t", f"v_done_{gi}", t_v)

        wait("scalar", f"qk_done_{gi}")

        def s_qk_evac(e):
            for i in range(4):
                ins = e.activation(self.qraw[i][:], bank(i), AF.Copy, scale=(0.125 if i < 2 else 1.0))
            return ins
        step("scalar", "s", f"qk_evac_{gi}", s_qk_evac)

        wait("scalar", f"v_done_{gi}")

        def s_v_evac(e):
            dstv = self.vaug[:].rearrange("p (r h c) -> p r h c", r=RING, h=16)
            for nb in range(2):
                ins = e.activation(dstv[:, vslot, nb * 8:(nb + 1) * 8, 0:64], bank(4 + nb), AF.Copy)
            return ins
        step("scalar", "s", f"v_evac_{gi}", s_v_evac)

    def _emit_rope(self, c):
        gi, m, vslot = c["gi"], c["m"], c["vslot"]
        step, wait = self.step, self.wait
        wait("vector", f"qk_evac_{gi}")
        if gi == 0:
            wait("vector", "tables")
        tcos = self.cos_sb[:, m * 128:(m + 1) * 128].unsqueeze(1).broadcast_to([128, 4, 128])
        tsin = self.sin_sb[:, m * 128:(m + 1) * 128].unsqueeze(1).broadcast_to([128, 4, 128])

        def v_rope(e):
            g4 = lambda ap: ap.rearrange("p (g t) -> p g t", g=4)
            for qk in range(2):
                x1 = g4(self.qraw[2 * qk][:])
                x2 = g4(self.qraw[2 * qk + 1][:])
                if qk == 0:
                    o1, o2 = g4(self.qrx[0][:]), g4(self.qrx[1][:])
                else:
                    o1 = g4(self.krx[0][:, vslot * 512:(vslot + 1) * 512])
                    o2 = g4(self.krx[1][:, vslot * 512:(vslot + 1) * 512])
                t1, t2 = g4(self.rp_t[0][:]), g4(self.rp_t[1][:])
                e.drain()
                e.tensor_tensor(t1, x1, tcos, ALU.mult)
                e.tensor_tensor(t2, x2, tsin, ALU.mult)
                e.drain()
                e.tensor_tensor(o1, t1, t2, ALU.subtract)
                e.drain()
                e.tensor_tensor(t1, x1, tsin, ALU.mult)
                e.tensor_tensor(t2, x2, tcos, ALU.mult)
                e.drain()
                ins = e.tensor_tensor(o2, t1, t2, ALU.add)
            return ins
        step("vector", "v", f"rope_{gi}", v_rope)

    SC_BANKS = [[0, 1, 2, 3], [4, 5, 6, 7], [5, 1, 2, 3], [7, 1, 2, 3]]
    AV_BANKS = [0, 4, 6, 5]

    def _emit_attention(self, c, nxt=None):
        gi, nkc, jr0, slots = c["gi"], c["nkc"], c["jr0"], c["slots"]
        step, wait, bank = self.step, self.wait, self.bank
        self.ln1_done_nxt = None
        expT_v = self.expT[:].rearrange("p (j h t) -> p j h t", j=3, h=H)
        vaug_v = self.vaug[:].rearrange("p (r hc) -> p r hc", r=RING)

        def emit_scores_head(w, i):
            h = 4 * w + i
            if w == 0:
                if h == 0:
                    wait("tensor", f"rope_{gi}")
                    wait("tensor", f"v_evac_{gi}")
                    if gi == 0:
                        wait("tensor", "v_prologue")
                        wait("tensor", "tables")
                if gi > 0:
                    wait("tensor", f"exphead_{gi - 1}_{12 + i}")
            else:
                wait("tensor", f"exphead_{gi}_{h - 4}")

            def t_sc(e, h=h, i=i):
                strip = i * 32
                g = h // 4
                e.matmul(bank(i, 0, nkc * 128), self.idbf[:],
                         self.mask3[:, jr0 * 128:384], start=True, stop=False)
                for jr in range(nkc):
                    sl = slots[jr]
                    for par in range(2):
                        ins = e.matmul(bank(i, jr * 128, (jr + 1) * 128),
                                       self.krx[par][strip:strip + 32,
                                                     sl * 512 + g * 128: sl * 512 + (g + 1) * 128],
                                       self.qrx[par][strip:strip + 32, g * 128:(g + 1) * 128],
                                       start=False, stop=(par == 1 and jr == nkc - 1),
                                       tile_position=(strip, 0))
                return ins
            step("tensor", "t", f"schead_{gi}_{h}", t_sc)

            wait("scalar", f"schead_{gi}_{h}")

            def s_exp(e, h=h, i=i):
                return e.activation(expT_v[:, jr0:3, h, :], bank(i, 0, nkc * 128), AF.Exp)
            step("scalar", "s", f"exphead_{gi}_{h}", s_exp)

        def emit_av(w):
            heads = tuple(range(4 * w, 4 * w + 4))
            if w >= 2:
                wait("tensor", f"aoev_{gi}_w{w - 2}")
            for h in range(4 * w, 4 * w + 4):
                wait("tensor", f"exphead_{gi}_{h}")

                def t_av(e, w=w, h=h):
                    ab = 4 + (w % 2)
                    off = (h % 4) * 128
                    for jr in range(nkc):
                        ins = e.matmul(bank(ab, off, off + 65),
                                       expT_v[:, jr0 + jr, h, :],
                                       vaug_v[:, slots[jr], h * 65:(h + 1) * 65],
                                       start=(jr == 0), stop=(jr == nkc - 1))
                    return ins
                step("tensor", "t",
                     f"av_{gi}_w{w}" if h == 4 * w + 3 else None, t_av)

            wait("vector", f"av_{gi}_w{w}")

            def v_div(e, w=w, heads=heads):
                ab = 4 + (w % 2)
                dens = self.ps_all[:, ab * 512:(ab + 1) * 512].rearrange(
                    "p (h c) -> p h c", h=4)[:, :, 64:65]
                e.reciprocal(self.rd_sb[:, 4 * w:4 * w + 4].rearrange(
                    "p (h c) -> p h c", c=1), dens)
                e.drain()
                for h in heads:
                    off = (h % 4) * 128
                    ins = e.tensor_scalar(self.attn_out[:, h * 64:(h + 1) * 64],
                                          bank(ab, off, off + 64),
                                          self.rd_sb[:, h:h + 1], None, ALU.mult)
                return ins
            step("vector", "v", f"div_{gi}_w{w}", v_div)
            if gi == 0:
                self._pump(4)

        def emit_trans_proj(w):
            wait("tensor", f"div_{gi}_w{w}")
            abbf = bank(4 + (w % 2)).bitcast(BF16)

            def t_tr(e, w=w):
                for k in range(2):
                    cc = 2 * w + k
                    ins = e.transpose(abbf[:, k * 128:(k + 1) * 128],
                                      self.attn_out[:, cc * 128:(cc + 1) * 128], self.idbf[:])
                return ins
            step("tensor", "t", f"aotr_{gi}_w{w}", t_tr)

            wait("scalar", f"aotr_{gi}_w{w}")

            def s_ev(e, w=w):
                return e.activation(self.attn_outT[:, w * 256:(w + 1) * 256],
                                    abbf[:, 0:256], AF.Copy)
            step("scalar", "s", f"aoev_{gi}_w{w}", s_ev)

            wait("tensor", f"aoev_{gi}_w{w}")
            if gi == 0:
                wait("tensor", f"wocast{2 * w + 1}")

            def t_proj(e, w=w):
                for nb in range(2):
                    for k in range(2):
                        cc = 2 * w + k
                        ins = e.matmul(bank(6 + nb),
                                       self.attn_outT[:, cc * 128:(cc + 1) * 128],
                                       self.woutT[:, cc * C + nb * 512: cc * C + (nb + 1) * 512],
                                       start=(w == 0 and k == 0), stop=(w == 3 and k == 1))
                return ins
            mile = f"proj_{gi}" if w == 3 else None
            step("tensor", "t", mile or f"projw_{gi}_w{w}", t_proj)

        for i in range(4):
            emit_scores_head(0, i)
        for i in range(4):
            emit_scores_head(1, i)
        emit_av(0)
        if nxt is not None:
            self.ln1_done_nxt = self._emit_ln1(nxt)
        for i in range(4):
            emit_scores_head(2, i)
        emit_av(1)
        emit_trans_proj(0)
        for i in range(4):
            emit_scores_head(3, i)
        emit_av(2)
        emit_trans_proj(1)
        emit_av(3)
        emit_trans_proj(2)
        emit_trans_proj(3)

    def _emit_tail(self, c, nxt=None):
        gi, xbuf, obuf = c["gi"], c["xbuf"], c["obuf"]
        step, wait, emit, bank = self.step, self.wait, self.emit, self.bank

        wait("vector", f"proj_{gi}")
        s1_pa = self.lnscr[:, (1 * 2 + gi % 2) * 8: (1 * 2 + gi % 2) * 8 + 1]

        if gi == 0:
            wait("vector", "boutld")

        def v_proj_b(e):
            return e.scalar_tensor_tensor(self.proj_b[:], self.ps_all[:, 6 * 512: 8 * 512], 1.0,
                                          self.boutB[:], ALU.mult, ALU.add, accum_out=s1_pa)
        step("vector", "v", f"proj_b_{gi}", v_proj_b)
        if gi == 0:
            self._pump(6)

        if nxt:
            self._emit_ln1T(nxt, self.ln1_done_nxt)

        wait("scalar", f"proj_b_{gi}")
        tpa_done = self._emit_ln(gi, 1, self.proj_b, self.t_pa, s1_pre=f"proj_b_{gi}")

        s1_x2 = self.lnscr[:, (2 * 2 + gi % 2) * 8: (2 * 2 + gi % 2) * 8 + 1]
        wait("vector", tpa_done)
        if gi > 0:
            wait("vector", f"x3_{gi - 1}")

        def v_x2(e):
            return e.scalar_tensor_tensor(self.x2_sb[:], xbuf[:], 1.0, self.t_pa[:],
                                          ALU.mult, ALU.add, accum_out=s1_x2)
        step("vector", "v", f"x2_{gi}", v_x2)
        if gi == 0:
            self._pump(99)

        if gi + 2 < self.n_tiles:
            wait("sync", f"x2_{gi}")
            self.dma_step("sync", "x", f"x_tile{gi + 2}",
                          lambda e, g=gi + 2: e.dma_start(out=self.x_in[g % 2][:],
                                                          in_=self.x_h[g * 128:(g + 1) * 128, :]))

        wait("scalar", f"x2_{gi}")
        ln2_done = self._emit_ln(gi, 2, self.x2_sb, self.ln2_f8, s1_pre=f"x2_{gi}")

        def v_ln2bf(e):
            e.drain()
            return e.tensor_copy(self.ln2_bf[:], self.ln2_f8[:])
        step("vector", "v", f"ln2bf_{gi}", v_ln2bf)

        if nxt:
            self._emit_qkv(nxt)
            self._emit_rope(nxt)

        wait("sync", f"ln2bf_{gi}")

        def d_ln2T_a(e):
            return e.dma_start(out=self.ln2T[:, 0:512].rearrange("p (c t) -> p c t", c=4),
                               in_=self.ln2_bf[:, 0:512].rearrange("p (c t) -> p c t", c=4),
                               transpose=True)
        self.dma_step("sync", "tx", f"ln2T_{gi}a", d_ln2T_a)

        def d_ln2T_b(e):
            return e.dma_start(out=self.ln2T[:, 512:1024].rearrange("p (c t) -> p c t", c=4),
                               in_=self.ln2_bf[:, 512:1024].rearrange("p (c t) -> p c t", c=4),
                               transpose=True)
        self.dma_step("sync", "tx", f"ln2T_{gi}b", d_ln2T_b)

        def mm_ffn(e, cch, nb):
            return e.matmul(bank(6 + nb),
                            self.ln2T[:, cch * 128:(cch + 1) * 128],
                            self.wffnT[:, cch * C + nb * 512: cch * C + (nb + 1) * 512],
                            start=(cch == 0), stop=(cch == 7))

        wait("tensor", f"ln2T_{gi}a")
        wait("tensor", f"proj_b_{gi}")
        if gi == 0:
            wait("tensor", "wfcast7")

        def t_ffn_a(e):
            for cch in range(4):
                for nb in range(2):
                    ins = mm_ffn(e, cch, nb)
            return ins
        step("tensor", "t", None, t_ffn_a)
        wait("tensor", f"ln2T_{gi}b")

        def t_ffn_b(e):
            for cch in range(4, 8):
                for nb in range(2):
                    ins = mm_ffn(e, cch, nb)
            return ins
        step("tensor", "t", f"ffn_{gi}", t_ffn_b)

        wait("vector", f"ffn_{gi}")
        s1_f = self.lnscr[:, (3 * 2 + gi % 2) * 8: (3 * 2 + gi % 2) * 8 + 1]

        if gi == 0:
            wait("vector", "bffnld")

        def v_ffn_b(e):
            return e.scalar_tensor_tensor(self.ffn_b[:], self.ps_all[:, 6 * 512: 8 * 512], 1.0,
                                          self.bffnB[:], ALU.mult, ALU.add, accum_out=s1_f)
        step("vector", "v", f"ffn_b_{gi}", v_ffn_b)

        wait("scalar", f"ffn_b_{gi}")
        if gi > 0:
            wait("scalar", f"x3_{gi - 1}")
        tf_done = self._emit_ln(gi, 3, self.ffn_b, self.t_ffn, s1_pre=f"ffn_b_{gi}")

        wait("gpsimd", tf_done)
        if gi >= 2:
            emit("gpsimd", lambda e, v=16 * (gi - 1): e.wait_ge(self.sems["st"], v))

        def g_x3(e):
            return e.tensor_tensor(obuf[:], self.x2_sb[:], self.t_ffn[:], ALU.add)
        step("gpsimd", "g", f"x3_{gi}", g_x3)
        wait("gpsimd", f"x3_{gi}")

        def g_store(e):
            return e.dma_start(out=self.out_h[gi * 128:(gi + 1) * 128, :], in_=obuf[:])
        self.dma_step("gpsimd", "st", f"store_{gi}", g_store)


def build_program(b_core=B_CORE, nt=NT, debug=False):
    return _Builder(b_core, nt, debug=debug).build()


DBG_TAPS = ("ln1_bf", "ln1T", "qraw0", "qraw1", "qraw2", "qraw3", "qrx0", "qrx1",
            "krx0", "krx1", "vaug", "expT", "attn_out", "attn_outT", "proj_b",
            "t_pa", "x2_sb", "ln2_f32", "ln2T_f8", "ffn_b", "t_ffn")


def host_inputs(w_qkv, w_out, b_out, w_ffn, b_ffn, t_seq=T_SEQ):
    perm = _qkv_row_perm()
    wqkvT = np.ascontiguousarray(w_qkv[perm].T)
    woutT = np.ascontiguousarray(w_out.T)
    wffnT = np.ascontiguousarray(w_ffn.T)
    boutB = np.ascontiguousarray(np.broadcast_to(b_out[None, :], (128, C)))
    bffnB = np.ascontiguousarray(np.broadcast_to(b_ffn[None, :], (128, C)))
    cos_t, sin_t = _rope_tables(t_seq)
    kk = np.arange(128)[:, None]
    qq = np.arange(128)[None, :]
    return {
        "wqkvT": wqkvT, "woutT": woutT, "wffnT": wffnT,
        "boutB": boutB, "bffnB": bffnB,
        "ropecos": cos_t, "ropesin": sin_t,
        "mask3": np.concatenate([
            np.where(kk >= qq, 0.0, -1e5),
            np.zeros((128, 128)),
            np.where(kk <= qq, 0.0, -1e5)], axis=1).astype(ml_dtypes.bfloat16),
        "identbf": np.eye(128, dtype=ml_dtypes.bfloat16),
    }


def kernel(L0_attn_x, L0_attn_w_qkv, L0_attn_w_out, L0_attn_b_out,
           L0_ffn_weight, L0_ffn_bias):
    x = np.ascontiguousarray(np.asarray(L0_attn_x, dtype=np.float32))
    shared = host_inputs(np.asarray(L0_attn_w_qkv, dtype=np.float32),
                         np.asarray(L0_attn_w_out, dtype=np.float32),
                         np.asarray(L0_attn_b_out, dtype=np.float32),
                         np.asarray(L0_ffn_weight, dtype=np.float32),
                         np.asarray(L0_ffn_bias, dtype=np.float32))
    nc = build_program()
    in_maps = []
    for c in range(N_CORES):
        im = dict(shared)
        im["x"] = np.ascontiguousarray(
            x[c * B_CORE:(c + 1) * B_CORE].reshape(B_CORE * T_SEQ, C))
        in_maps.append(im)
    res = run_bass_kernel_spmd(nc, in_maps, core_ids=list(range(N_CORES)))
    out = np.concatenate([r["out"].reshape(B_CORE, T_SEQ, C) for r in res.results], axis=0)
    return out.astype(np.float32)


if __name__ == "__main__":
    build_program()
    print("built ok")


# version 41
# speedup vs baseline: 1.0984x; 1.0093x over previous
import numpy as np
import ml_dtypes

import concourse.bass as bass
import concourse.mybir as mybir
from concourse.bass_utils import run_bass_kernel_spmd

F32 = mybir.dt.float32
BF16 = mybir.dt.bfloat16
F8 = mybir.dt.float8e4
AF = mybir.ActivationFunctionType
ALU = mybir.AluOpType

N_CORES = 8
B_FULL = 16
T_SEQ = 2048
C = 1024
H = 16
D = 64
QT = 128
EPS = 1e-5
RING = 4

B_CORE = B_FULL // N_CORES
NT = T_SEQ // QT


def _qkv_row_perm():
    rows = []
    for base, par in ((0, 0), (0, 1), (64, 0), (64, 1)):
        for h in range(H):
            for j in range(32):
                rows.append(h * 192 + base + 2 * j + par)
    for h in range(H):
        for d in range(D):
            rows.append(h * 192 + 128 + d)
    assert len(rows) == 3 * C and len(set(rows)) == 3 * C
    return np.asarray(rows, dtype=np.int64)


def _rope_tables(t_seq):
    d2 = D // 2
    theta = (10000.0 ** (-(np.arange(d2, dtype=np.float32) / np.float32(d2)))).astype(np.float32)
    ang32 = np.arange(t_seq, dtype=np.float32)[:, None] * theta[None, :]
    cosv = np.cos(ang32.astype(np.float64)).astype(np.float32)
    sinv = np.sin(ang32.astype(np.float64)).astype(np.float32)
    p = np.arange(128) % d2
    return (np.ascontiguousarray(cosv[:, p].T).astype(ml_dtypes.bfloat16),
            np.ascontiguousarray(sinv[:, p].T).astype(ml_dtypes.bfloat16))


class _Builder:
    def __init__(self, b_core, nt, debug=False):
        self.debug = debug
        self.b_core, self.nt = b_core, nt
        self.t_seq = nt * QT
        self.n_tiles = b_core * nt
        self.queues = {k: [] for k in ("sync", "gpsimd", "vector", "scalar", "tensor")}
        self.counts = {k: 0 for k in ("x", "tx", "ty", "wg", "wh", "wi", "wj", "wo", "wf", "st", "t", "v", "s", "g")}
        self.miles = {}
        self.sems = {}

    def emit(self, eng, fn):
        self.queues[eng].append(fn)

    def wait(self, eng, milestone):
        sem_name, val = self.miles[milestone]
        self.emit(eng, lambda e, sn=sem_name, v=val: e.wait_ge(self.sems[sn], v))

    def step(self, eng, sem_name, milestone, fn, by=1):
        def run(e, f=fn, sn=sem_name, b=by):
            ins = f(e)
            ins.then_inc(self.sems[sn], b)
        self.emit(eng, run)
        self.counts[sem_name] += by
        if milestone is not None:
            self.miles[milestone] = (sem_name, self.counts[sem_name])

    def dma_step(self, eng, sem_name, milestone, fn):
        prev = self.counts[sem_name]
        if prev:
            self.emit(eng, lambda e, sn=sem_name, v=prev: e.wait_ge(self.sems[sn], v))
        self.step(eng, sem_name, milestone, fn, by=16)

    def build(self):
        nc = bass.Bass("TRN2", target_bir_lowering=False)
        self.nc = nc
        t_seq, nt = self.t_seq, self.nt

        self.x_h = nc.dram_tensor("x", [self.b_core * t_seq, C], F32, kind="ExternalInput")
        self.wqkv_h = nc.dram_tensor("wqkvT", [C, 3 * C], F32, kind="ExternalInput")
        self.wout_h = nc.dram_tensor("woutT", [C, C], F32, kind="ExternalInput")
        self.wffn_h = nc.dram_tensor("wffnT", [C, C], F32, kind="ExternalInput")
        self.bout_h = nc.dram_tensor("boutB", [128, C], F32, kind="ExternalInput")
        self.bffn_h = nc.dram_tensor("bffnB", [128, C], F32, kind="ExternalInput")
        self.cos_h = nc.dram_tensor("ropecos", [128, t_seq], BF16, kind="ExternalInput")
        self.sin_h = nc.dram_tensor("ropesin", [128, t_seq], BF16, kind="ExternalInput")
        self.mask3_h = nc.dram_tensor("mask3", [128, 384], BF16, kind="ExternalInput")
        self.idbf_h = nc.dram_tensor("identbf", [128, 128], BF16, kind="ExternalInput")
        self.out_h = nc.dram_tensor("out", [self.b_core * t_seq, C], F32, kind="ExternalOutput")

        from contextlib import ExitStack
        es = ExitStack()
        sbuf = lambda n, sh, dt: es.enter_context(nc.sbuf_tensor(n, sh, dt))

        self.wqkvT = sbuf("wqkvT_sb", [128, 8 * 3 * C], BF16)
        self.woutT = sbuf("woutT_sb", [128, 8 * C], BF16)
        self.wffnT = sbuf("wffnT_sb", [128, 8 * C], BF16)
        self.wf8 = sbuf("wf8_sb", [128, C], F8)
        self.wstg = [sbuf(f"wstg{i}", [128, C], F32) for i in range(4)]
        self.cos_sb = sbuf("cos_sb", [128, t_seq], BF16)
        self.sin_sb = sbuf("sin_sb", [128, t_seq], BF16)
        self.boutB = sbuf("boutB_sb", [128, C], F32)
        self.bffnB = sbuf("bffnB_sb", [128, C], F32)
        self.mask3 = sbuf("mask3_sb", [128, 384], BF16)
        self.idbf = sbuf("idbf_sb", [128, 128], BF16)
        self.x_in = [sbuf(f"x_in{i}", [128, C], F32) for i in range(2)]
        self.x3_out = [sbuf(f"x3_out{i}", [128, C], F32) for i in range(2)]
        self.ln1_bf = sbuf("ln1_bf", [128, C], BF16)
        self.ln1T = sbuf("ln1T", [128, C], BF16)
        self.qraw = [sbuf(f"qraw{i}", [128, 512], BF16) for i in range(4)]
        self.rp_t = [sbuf(f"rp_t{i}", [128, 512], BF16) for i in range(2)]
        self.qrx = [sbuf(f"qrx{i}", [128, 512], BF16) for i in range(2)]
        self.krx = [sbuf(f"krx{i}", [128, RING * 512], BF16) for i in range(2)]
        self.vaug = sbuf("vaug", [128, RING * 16 * 65], BF16)
        self.expT = sbuf("expT", [128, 3 * H * 128], BF16)
        self.attn_out = sbuf("attn_out", [128, C], BF16)
        self.attn_outT = sbuf("attn_outT", [128, C], BF16)
        self.rd_sb = sbuf("rd_sb", [128, H], F32)
        self.proj_b = sbuf("proj_b", [128, C], F32)
        self.t_pa = sbuf("t_pa", [128, C], F32)
        self.x2_sb = sbuf("x2_sb", [128, C], F32)
        self.ln2_f8 = sbuf("ln2_f8", [128, C], F8)
        self.ln2_bf = sbuf("ln2_bf", [128, C], BF16)
        self.ln2T = sbuf("ln2T_bf", [128, C], BF16)
        self.ffn_b = sbuf("ffn_b", [128, C], F32)
        self.t_ffn = sbuf("t_ffn", [128, C], F32)
        self.sq_scr = sbuf("sq_scr", [128, C], BF16)
        self.lnscr = sbuf("lnscr", [128, 4 * 2 * 8], F32)
        self.dbg_expin = sbuf("dbg_expin", [128, 3 * 128], F32)

        self.ps_all = es.enter_context(nc.psum_tensor("ps_all", [128, 4096], F32))
        for n in ("x", "tx", "ty", "wg", "wh", "wi", "wj", "wo", "wf", "st", "t", "v", "s", "g"):
            self.sems[n] = es.enter_context(nc.semaphore(f"sem_{n}"))

        c0 = self._ctx(0)
        self._prologue(c0)
        self._emit_qkv(c0)
        self._emit_rope(c0)
        self._pump(4)
        for gi in range(self.n_tiles):
            c = self._ctx(gi)
            nxt = self._ctx(gi + 1) if gi + 1 < self.n_tiles else None
            self._emit_attention(c, nxt)
            self._emit_tail(c, nxt)
        if self.debug:
            self._emit_debug_taps()
        total_st = self.counts["st"]
        self.emit("gpsimd", lambda e, v=total_st: e.wait_ge(self.sems["st"], v))

        with es, nc.Block() as block:
            @block.sync
            def _(eng):
                for fn in self.queues["sync"]:
                    fn(eng)

            @block.gpsimd
            def _(eng):
                for fn in self.queues["gpsimd"]:
                    fn(eng)

            @block.vector
            def _(eng):
                for fn in self.queues["vector"]:
                    fn(eng)

            @block.scalar
            def _(eng):
                for fn in self.queues["scalar"]:
                    fn(eng)

            @block.tensor
            def _(eng):
                for fn in self.queues["tensor"]:
                    fn(eng)
        return nc

    def bank(self, b, lo=0, hi=512):
        return self.ps_all[:, b * 512 + lo: b * 512 + hi]

    def _prologue(self, c0):
        step, wait = self.step, self.wait

        def ld(eng, sem, dst, src, mile=None):
            self.dma_step(eng, sem, mile,
                 lambda e, d=dst, s=src: e.dma_start(out=d[:], in_=s[:, :]))
        ld("sync", "x", self.x_in[0], self.x_h[0:128], "x_tile0")
        if self.n_tiles > 1:
            ld("sync", "x", self.x_in[1], self.x_h[128:256], "x_tile1")
        ln1_done0 = self._emit_ln1(c0)
        self._emit_ln1T(c0, ln1_done0)
        for dst, src in ((self.cos_sb, self.cos_h), (self.sin_sb, self.sin_h),
                         (self.idbf, self.idbf_h), (self.mask3, self.mask3_h)):
            ld("scalar", "wh", dst, src)
        self.miles["tables"] = ("wh", self.counts["wh"])

        jobs = []
        for hf in range(3):
            for cch in range(8):
                src = self.wqkv_h[cch * 128:(cch + 1) * 128, hf * C:(hf + 1) * C]
                q = "scalar" if (hf == 2 and cch >= 4) else "gpsimd"
                jobs.append((q, f"wqkv{cch}_{hf}", src, cch * 3 * C + hf * C))
        qsems = {"gpsimd": ("wg", "wi"), "scalar": ("wh", "wj"), "sync": ("wh", "wj")}
        qn = {"gpsimd": 0, "scalar": 0, "sync": 0}
        for idx, (q, name, src, off) in enumerate(jobs):
            if idx >= 4:
                wait(q, f"wcast{idx - 4}")
            sem = qsems[q][qn[q] % 2]
            qn[q] += 1
            self.dma_step(q, sem, name,
                          lambda e, s_=src, b_=idx % 4: e.dma_start(
                              out=self.wstg[b_][:, :], in_=s_))
            wait("vector", name)

            def v_cast(e, o_=off, b_=idx % 4):
                e.drain()
                return e.tensor_copy(self.wqkvT[:, o_:o_ + C], self.wstg[b_][:, :])
            step("vector", "v", f"wcast{idx}", v_cast)
        sem = qsems["gpsimd"][qn["gpsimd"] % 2]
        self.dma_step("gpsimd", sem, "boutld",
                      lambda e: e.dma_start(out=self.boutB[:], in_=self.bout_h[:, :]))
        sem = qsems["gpsimd"][(qn["gpsimd"] + 1) % 2]
        self.dma_step("gpsimd", sem, "bffnld",
                      lambda e: e.dma_start(out=self.bffnB[:], in_=self.bffn_h[:, :]))

        def v_prologue(e):
            ones = self.vaug[:].rearrange("p (r h c) -> p r h c", r=RING, h=16)
            return e.memset(ones[:, :, :, 64:65], 1.0)
        step("vector", "v", "v_prologue", v_prologue)

        def mk_load_wo(m):
            def go():
                self.wait("sync", f"wcast{20 + m}" if m < 4 else f"wocast{m - 4}")
                sem = ("wh", "wj")[m % 2]
                self.dma_step("sync", sem, f"wo{m}",
                              lambda e, m=m: e.dma_start(
                                  out=self.wstg[m % 4][:, :],
                                  in_=self.wout_h[m * 128:(m + 1) * 128, :]))
            return go

        def mk_cast_wo(m):
            def go():
                self.wait("vector", f"wo{m}")

                def v_c(e, m=m):
                    e.drain()
                    return e.tensor_copy(self.woutT[:, m * C:(m + 1) * C],
                                         self.wstg[m % 4][:, :])
                self.step("vector", "v", f"wocast{m}", v_c)
            return go

        def mk_load_wf(m):
            def go():
                self.wait("gpsimd", f"wocast{4 + m}" if m < 4 else f"wfcast{m - 4}")
                sem = ("wg", "wi")[m % 2]
                self.dma_step("gpsimd", sem, f"wf{m}",
                              lambda e, m=m: e.dma_start(
                                  out=self.wstg[m % 4][:, :],
                                  in_=self.wffn_h[m * 128:(m + 1) * 128, :]))
            return go

        def mk_cast_wf(m):
            def go():
                self.wait("vector", f"wf{m}")

                def v_c(e, m=m):
                    e.drain()
                    e.tensor_copy(self.wf8[:], self.wstg[m % 4][:, :])
                    e.drain()
                    return e.tensor_copy(self.wffnT[:, m * C:(m + 1) * C], self.wf8[:])
                self.step("vector", "v", f"wfcast{m}", v_c)
            return go

        p = [mk_load_wo(0), mk_load_wo(1)]
        for m in range(2, 8):
            p += [mk_cast_wo(m - 2), mk_load_wo(m)]
        p += [mk_cast_wo(6), mk_cast_wo(7), mk_load_wf(0), mk_load_wf(1)]
        for m in range(2, 8):
            p += [mk_cast_wf(m - 2), mk_load_wf(m)]
        p += [mk_cast_wf(6), mk_cast_wf(7)]
        self.pending = p

    def _pump(self, n):
        while n and self.pending:
            self.pending.pop(0)()
            n -= 1

    def _emit_debug_taps(self):
        tensors = {
            "ln1_bf": self.ln1_bf, "ln1T": self.ln1T,
            "qraw0": self.qraw[0], "qraw1": self.qraw[1],
            "qraw2": self.qraw[2], "qraw3": self.qraw[3],
            "qrx0": self.qrx[0], "qrx1": self.qrx[1],
            "krx0": self.krx[0], "krx1": self.krx[1],
            "vaug": self.vaug, "expT": self.expT,
            "attn_out": self.attn_out, "attn_outT": self.attn_outT,
            "proj_b": self.proj_b, "t_pa": self.t_pa, "x2_sb": self.x2_sb,
            "ln2_f32": self.ln2_f8, "ln2T_f8": self.ln2T, "ffn_b": self.ffn_b,
            "t_ffn": self.t_ffn,
        }
        for name in ("t", "v", "s", "g"):
            self.emit("gpsimd",
                      lambda e, sn=name, v=self.counts[name]: e.wait_ge(self.sems[sn], v))
        for name, t in tensors.items():
            h = self.nc.dram_tensor("dbg_" + name, list(t.shape), t.dtype, kind="ExternalOutput")
            self.dma_step("gpsimd", "st", None,
                          lambda e, hh=h, tt=t: e.dma_start(out=hh[:, :], in_=tt[:]))

    def _emit_ln(self, gi, phase, src_sb, dst_sb, s1_pre=None):
        step, wait = self.step, self.wait
        base = (phase * 2 + gi % 2) * 8
        L = self.lnscr
        s1 = L[:, base + 0: base + 1]
        mean = L[:, base + 1: base + 2]
        msq = L[:, base + 2: base + 3]
        var = L[:, base + 3: base + 4]
        rin = L[:, base + 4: base + 5]
        rstd = L[:, base + 5: base + 6]
        nm = L[:, base + 6: base + 7]
        s2 = L[:, base + 7: base + 8]
        tag = f"ln{phase}_{gi}"

        def s_sq(e):
            return e.activation(self.sq_scr[:], src_sb[:], AF.Square, accum_out=s2)
        step("scalar", "s", f"{tag}_sq", s_sq)

        if s1_pre is None:
            def v_head(e):
                e.reduce_sum(s1, src_sb[:], axis=mybir.AxisListType.X)
                e.drain()
                return e.nop()
            self.emit("vector", v_head)
        else:
            wait("vector", s1_pre)

        def v_stats(e):
            e.tensor_scalar_mul(mean, s1, 1.0 / C)
            e.tensor_scalar(msq, s1, s1, 1.0 / (C * C), ALU.mult, ALU.mult)
            return e.nop()
        self.emit("vector", v_stats)
        wait("vector", f"{tag}_sq")

        def v_stats2(e):
            e.tensor_scalar(var, s2, 1.0 / C, EPS, ALU.mult, ALU.add)
            e.drain()
            return e.tensor_scalar(var, var, msq, None, ALU.subtract)
        step("vector", "v", f"{tag}_rin", v_stats2)

        wait("scalar", f"{tag}_rin")

        def s_rstd(e):
            e.activation(rin, var, AF.Ln)
            e.drain()
            return e.activation(rstd, rin, AF.Exp, scale=-0.5)
        step("scalar", "s", f"{tag}_rstd", s_rstd)

        wait("vector", f"{tag}_rstd")

        def v_nm(e):
            e.drain()
            return e.tensor_scalar(nm, mean, rstd, -1.0, ALU.mult, ALU.mult)
        step("vector", "v", f"{tag}_nm", v_nm)

        def v_norm(e):
            e.drain()
            return e.tensor_scalar(dst_sb[:], src_sb[:], rstd, nm, ALU.mult, ALU.add)
        step("vector", "v", f"{tag}_done", v_norm)
        return f"{tag}_done"

    def _ctx(self, gi):
        b, m = divmod(gi, self.nt)
        kcs = [j for j in (m - 2, m - 1, m) if j >= 0]
        return {
            "gi": gi, "b": b, "m": m, "kcs": kcs, "nkc": len(kcs),
            "jr0": 3 - len(kcs),
            "slots": [(b * self.nt + j) % RING for j in kcs],
            "vslot": (b * self.nt + m) % RING,
            "xbuf": self.x_in[gi % 2],
            "obuf": self.x3_out[gi % 2],
        }

    def _emit_ln1(self, c):
        gi = c["gi"]
        self.wait("vector", f"x_tile{gi}")
        self.wait("scalar", f"x_tile{gi}")
        return self._emit_ln(gi, 0, c["xbuf"], self.ln1_bf)

    def _emit_ln1T(self, c, ln1_done):
        gi = c["gi"]
        self.wait("sync", ln1_done)

        def d_ln1T(e):
            return e.dma_start(out=self.ln1T[:].rearrange("p (c t) -> p c t", c=8),
                               in_=self.ln1_bf[:].rearrange("p (c t) -> p c t", c=8),
                               transpose=True)
        self.dma_step("sync", "tx", f"ln1T_{gi}", d_ln1T)

    def _emit_qkv(self, c):
        gi = c["gi"]
        step, wait, bank = self.step, self.wait, self.bank
        vslot = c["vslot"]
        wait("tensor", f"ln1T_{gi}")
        if gi == 0:
            wait("tensor", "wcast7")

        def mk_qk(cgs, mile):
            def t_qk(e, cgs=cgs):
                for cg in cgs:
                    bnk, off = cg // 4, (cg % 4) * 128
                    for kc in range(8):
                        ins = e.matmul(bank(bnk, off, off + 128),
                                       self.wqkvT[:, kc * 3 * C + cg * 128: kc * 3 * C + (cg + 1) * 128],
                                       self.ln1T[:, kc * 128:(kc + 1) * 128],
                                       start=(kc == 0), stop=(kc == 7))
                return ins
            step("tensor", "t", mile, t_qk)
        if gi == 0:
            mk_qk(range(8), "qkq_0")
            wait("tensor", "wcast15")
            mk_qk(range(8, 16), f"qk_done_{gi}")
            wait("tensor", "wcast23")
        else:
            mk_qk(range(16), f"qk_done_{gi}")

        def t_v(e):
            for nb in range(2):
                for kc in range(8):
                    ins = e.matmul(bank(4 + nb),
                                   self.ln1T[:, kc * 128:(kc + 1) * 128],
                                   self.wqkvT[:, kc * 3 * C + 2048 + nb * 512: kc * 3 * C + 2048 + (nb + 1) * 512],
                                   start=(kc == 0), stop=(kc == 7))
            return ins
        step("tensor", "t", f"v_done_{gi}", t_v)

        wait("scalar", f"qk_done_{gi}")

        def s_qk_evac(e):
            for i in range(4):
                ins = e.activation(self.qraw[i][:], bank(i), AF.Copy, scale=(0.125 if i < 2 else 1.0))
            return ins
        step("scalar", "s", f"qk_evac_{gi}", s_qk_evac)

        wait("scalar", f"v_done_{gi}")

        def s_v_evac(e):
            dstv = self.vaug[:].rearrange("p (r h c) -> p r h c", r=RING, h=16)
            for nb in range(2):
                ins = e.activation(dstv[:, vslot, nb * 8:(nb + 1) * 8, 0:64], bank(4 + nb), AF.Copy)
            return ins
        step("scalar", "s", f"v_evac_{gi}", s_v_evac)

    def _emit_rope(self, c):
        gi, m, vslot = c["gi"], c["m"], c["vslot"]
        step, wait = self.step, self.wait
        wait("vector", f"qk_evac_{gi}")
        if gi == 0:
            wait("vector", "tables")
        tcos = self.cos_sb[:, m * 128:(m + 1) * 128].unsqueeze(1).broadcast_to([128, 4, 128])
        tsin = self.sin_sb[:, m * 128:(m + 1) * 128].unsqueeze(1).broadcast_to([128, 4, 128])

        def v_rope(e):
            g4 = lambda ap: ap.rearrange("p (g t) -> p g t", g=4)
            for qk in range(2):
                x1 = g4(self.qraw[2 * qk][:])
                x2 = g4(self.qraw[2 * qk + 1][:])
                if qk == 0:
                    o1, o2 = g4(self.qrx[0][:]), g4(self.qrx[1][:])
                else:
                    o1 = g4(self.krx[0][:, vslot * 512:(vslot + 1) * 512])
                    o2 = g4(self.krx[1][:, vslot * 512:(vslot + 1) * 512])
                t1, t2 = g4(self.rp_t[0][:]), g4(self.rp_t[1][:])
                e.drain()
                e.tensor_tensor(t1, x1, tcos, ALU.mult)
                e.tensor_tensor(t2, x2, tsin, ALU.mult)
                e.drain()
                e.tensor_tensor(o1, t1, t2, ALU.subtract)
                e.drain()
                e.tensor_tensor(t1, x1, tsin, ALU.mult)
                e.tensor_tensor(t2, x2, tcos, ALU.mult)
                e.drain()
                ins = e.tensor_tensor(o2, t1, t2, ALU.add)
            return ins
        step("vector", "v", f"rope_{gi}", v_rope)

    SC_BANKS = [[0, 1, 2, 3], [4, 5, 6, 7], [5, 1, 2, 3], [7, 1, 2, 3]]
    AV_BANKS = [0, 4, 6, 5]

    def _emit_attention(self, c, nxt=None):
        gi, nkc, jr0, slots = c["gi"], c["nkc"], c["jr0"], c["slots"]
        step, wait, bank = self.step, self.wait, self.bank
        self.ln1_done_nxt = None
        expT_v = self.expT[:].rearrange("p (j h t) -> p j h t", j=3, h=H)
        vaug_v = self.vaug[:].rearrange("p (r hc) -> p r hc", r=RING)

        def emit_scores_head(w, i):
            h = 4 * w + i
            if w == 0:
                if h == 0:
                    wait("tensor", f"rope_{gi}")
                    wait("tensor", f"v_evac_{gi}")
                    if gi == 0:
                        wait("tensor", "v_prologue")
                        wait("tensor", "tables")
                if gi > 0:
                    wait("tensor", f"exphead_{gi - 1}_{12 + i}")
            else:
                wait("tensor", f"exphead_{gi}_{h - 4}")

            def t_sc(e, h=h, i=i):
                strip = i * 32
                g = h // 4
                e.matmul(bank(i, 0, nkc * 128), self.idbf[:],
                         self.mask3[:, jr0 * 128:384], start=True, stop=False)
                for jr in range(nkc):
                    sl = slots[jr]
                    for par in range(2):
                        ins = e.matmul(bank(i, jr * 128, (jr + 1) * 128),
                                       self.krx[par][strip:strip + 32,
                                                     sl * 512 + g * 128: sl * 512 + (g + 1) * 128],
                                       self.qrx[par][strip:strip + 32, g * 128:(g + 1) * 128],
                                       start=False, stop=(par == 1 and jr == nkc - 1),
                                       tile_position=(strip, 0))
                return ins
            step("tensor", "t", f"schead_{gi}_{h}", t_sc)

            wait("scalar", f"schead_{gi}_{h}")

            def s_exp(e, h=h, i=i):
                return e.activation(expT_v[:, jr0:3, h, :], bank(i, 0, nkc * 128), AF.Exp)
            step("scalar", "s", f"exphead_{gi}_{h}", s_exp)

        def emit_av(w):
            heads = tuple(range(4 * w, 4 * w + 4))
            if w >= 2:
                wait("tensor", f"aoev_{gi}_w{w - 2}")
            for h in range(4 * w, 4 * w + 4):
                wait("tensor", f"exphead_{gi}_{h}")

                def t_av(e, w=w, h=h):
                    ab = 4 + (w % 2)
                    off = (h % 4) * 128
                    for jr in range(nkc):
                        ins = e.matmul(bank(ab, off, off + 65),
                                       expT_v[:, jr0 + jr, h, :],
                                       vaug_v[:, slots[jr], h * 65:(h + 1) * 65],
                                       start=(jr == 0), stop=(jr == nkc - 1))
                    return ins
                step("tensor", "t",
                     f"av_{gi}_w{w}" if h == 4 * w + 3 else None, t_av)

            wait("vector", f"av_{gi}_w{w}")

            def v_div(e, w=w, heads=heads):
                ab = 4 + (w % 2)
                for h in heads:
                    off = (h % 4) * 128
                    e.reciprocal(self.rd_sb[:, h:h + 1], bank(ab, off + 64, off + 65))
                e.drain()
                for h in heads:
                    off = (h % 4) * 128
                    ins = e.tensor_scalar(self.attn_out[:, h * 64:(h + 1) * 64],
                                          bank(ab, off, off + 64),
                                          self.rd_sb[:, h:h + 1], None, ALU.mult)
                return ins
            step("vector", "v", f"div_{gi}_w{w}", v_div)
            if gi == 0:
                self._pump(4)

        def emit_trans_proj(w):
            wait("tensor", f"div_{gi}_w{w}")
            abbf = bank(4 + (w % 2)).bitcast(BF16)

            def t_tr(e, w=w):
                for k in range(2):
                    cc = 2 * w + k
                    ins = e.transpose(abbf[:, k * 128:(k + 1) * 128],
                                      self.attn_out[:, cc * 128:(cc + 1) * 128], self.idbf[:])
                return ins
            step("tensor", "t", f"aotr_{gi}_w{w}", t_tr)

            wait("scalar", f"aotr_{gi}_w{w}")

            def s_ev(e, w=w):
                return e.activation(self.attn_outT[:, w * 256:(w + 1) * 256],
                                    abbf[:, 0:256], AF.Copy)
            step("scalar", "s", f"aoev_{gi}_w{w}", s_ev)

            wait("tensor", f"aoev_{gi}_w{w}")
            if gi == 0:
                wait("tensor", f"wocast{2 * w + 1}")

            def t_proj(e, w=w):
                for nb in range(2):
                    for k in range(2):
                        cc = 2 * w + k
                        ins = e.matmul(bank(6 + nb),
                                       self.attn_outT[:, cc * 128:(cc + 1) * 128],
                                       self.woutT[:, cc * C + nb * 512: cc * C + (nb + 1) * 512],
                                       start=(w == 0 and k == 0), stop=(w == 3 and k == 1))
                return ins
            mile = f"proj_{gi}" if w == 3 else None
            step("tensor", "t", mile or f"projw_{gi}_w{w}", t_proj)

        for i in range(4):
            emit_scores_head(0, i)
        for i in range(4):
            emit_scores_head(1, i)
        emit_av(0)
        if nxt is not None:
            self.ln1_done_nxt = self._emit_ln1(nxt)
        for i in range(4):
            emit_scores_head(2, i)
        emit_av(1)
        emit_trans_proj(0)
        for i in range(4):
            emit_scores_head(3, i)
        emit_av(2)
        emit_trans_proj(1)
        emit_av(3)
        emit_trans_proj(2)
        emit_trans_proj(3)

    def _emit_tail(self, c, nxt=None):
        gi, xbuf, obuf = c["gi"], c["xbuf"], c["obuf"]
        step, wait, emit, bank = self.step, self.wait, self.emit, self.bank

        wait("vector", f"proj_{gi}")
        s1_pa = self.lnscr[:, (1 * 2 + gi % 2) * 8: (1 * 2 + gi % 2) * 8 + 1]

        if gi == 0:
            wait("vector", "boutld")

        def v_proj_b(e):
            return e.scalar_tensor_tensor(self.proj_b[:], self.ps_all[:, 6 * 512: 8 * 512], 1.0,
                                          self.boutB[:], ALU.mult, ALU.add, accum_out=s1_pa)
        step("vector", "v", f"proj_b_{gi}", v_proj_b)
        if gi == 0:
            self._pump(6)

        if nxt:
            self._emit_ln1T(nxt, self.ln1_done_nxt)

        wait("scalar", f"proj_b_{gi}")
        tpa_done = self._emit_ln(gi, 1, self.proj_b, self.t_pa, s1_pre=f"proj_b_{gi}")

        s1_x2 = self.lnscr[:, (2 * 2 + gi % 2) * 8: (2 * 2 + gi % 2) * 8 + 1]
        wait("vector", tpa_done)
        if gi > 0:
            wait("vector", f"x3_{gi - 1}")

        def v_x2(e):
            return e.scalar_tensor_tensor(self.x2_sb[:], xbuf[:], 1.0, self.t_pa[:],
                                          ALU.mult, ALU.add, accum_out=s1_x2)
        step("vector", "v", f"x2_{gi}", v_x2)
        if gi == 0:
            self._pump(99)

        if gi + 2 < self.n_tiles:
            wait("sync", f"x2_{gi}")
            self.dma_step("sync", "x", f"x_tile{gi + 2}",
                          lambda e, g=gi + 2: e.dma_start(out=self.x_in[g % 2][:],
                                                          in_=self.x_h[g * 128:(g + 1) * 128, :]))

        wait("scalar", f"x2_{gi}")
        ln2_done = self._emit_ln(gi, 2, self.x2_sb, self.ln2_f8, s1_pre=f"x2_{gi}")

        def v_ln2bf(e):
            e.drain()
            return e.tensor_copy(self.ln2_bf[:], self.ln2_f8[:])
        step("vector", "v", f"ln2bf_{gi}", v_ln2bf)

        if nxt:
            self._emit_qkv(nxt)
            self._emit_rope(nxt)

        wait("sync", f"ln2bf_{gi}")

        def d_ln2T_a(e):
            return e.dma_start(out=self.ln2T[:, 0:512].rearrange("p (c t) -> p c t", c=4),
                               in_=self.ln2_bf[:, 0:512].rearrange("p (c t) -> p c t", c=4),
                               transpose=True)
        self.dma_step("sync", "tx", f"ln2T_{gi}a", d_ln2T_a)

        def d_ln2T_b(e):
            return e.dma_start(out=self.ln2T[:, 512:1024].rearrange("p (c t) -> p c t", c=4),
                               in_=self.ln2_bf[:, 512:1024].rearrange("p (c t) -> p c t", c=4),
                               transpose=True)
        self.dma_step("sync", "tx", f"ln2T_{gi}b", d_ln2T_b)

        def mm_ffn(e, cch, nb):
            return e.matmul(bank(6 + nb),
                            self.ln2T[:, cch * 128:(cch + 1) * 128],
                            self.wffnT[:, cch * C + nb * 512: cch * C + (nb + 1) * 512],
                            start=(cch == 0), stop=(cch == 7))

        wait("tensor", f"ln2T_{gi}a")
        wait("tensor", f"proj_b_{gi}")
        if gi == 0:
            wait("tensor", "wfcast7")

        def t_ffn_a(e):
            for cch in range(4):
                for nb in range(2):
                    ins = mm_ffn(e, cch, nb)
            return ins
        step("tensor", "t", None, t_ffn_a)
        wait("tensor", f"ln2T_{gi}b")

        def t_ffn_b(e):
            for cch in range(4, 8):
                for nb in range(2):
                    ins = mm_ffn(e, cch, nb)
            return ins
        step("tensor", "t", f"ffn_{gi}", t_ffn_b)

        wait("vector", f"ffn_{gi}")
        s1_f = self.lnscr[:, (3 * 2 + gi % 2) * 8: (3 * 2 + gi % 2) * 8 + 1]

        if gi == 0:
            wait("vector", "bffnld")

        def v_ffn_b(e):
            return e.scalar_tensor_tensor(self.ffn_b[:], self.ps_all[:, 6 * 512: 8 * 512], 1.0,
                                          self.bffnB[:], ALU.mult, ALU.add, accum_out=s1_f)
        step("vector", "v", f"ffn_b_{gi}", v_ffn_b)

        wait("scalar", f"ffn_b_{gi}")
        if gi > 0:
            wait("scalar", f"x3_{gi - 1}")
        tf_done = self._emit_ln(gi, 3, self.ffn_b, self.t_ffn, s1_pre=f"ffn_b_{gi}")

        wait("gpsimd", tf_done)
        if gi >= 2:
            emit("gpsimd", lambda e, v=16 * (gi - 1): e.wait_ge(self.sems["st"], v))

        def g_x3(e):
            return e.tensor_tensor(obuf[:], self.x2_sb[:], self.t_ffn[:], ALU.add)
        step("gpsimd", "g", f"x3_{gi}", g_x3)
        wait("gpsimd", f"x3_{gi}")

        def g_store(e):
            return e.dma_start(out=self.out_h[gi * 128:(gi + 1) * 128, :], in_=obuf[:])
        self.dma_step("gpsimd", "st", f"store_{gi}", g_store)


def build_program(b_core=B_CORE, nt=NT, debug=False):
    return _Builder(b_core, nt, debug=debug).build()


DBG_TAPS = ("ln1_bf", "ln1T", "qraw0", "qraw1", "qraw2", "qraw3", "qrx0", "qrx1",
            "krx0", "krx1", "vaug", "expT", "attn_out", "attn_outT", "proj_b",
            "t_pa", "x2_sb", "ln2_f32", "ln2T_f8", "ffn_b", "t_ffn")


def host_inputs(w_qkv, w_out, b_out, w_ffn, b_ffn, t_seq=T_SEQ):
    perm = _qkv_row_perm()
    wqkvT = np.ascontiguousarray(w_qkv[perm].T)
    woutT = np.ascontiguousarray(w_out.T)
    wffnT = np.ascontiguousarray(w_ffn.T)
    boutB = np.ascontiguousarray(np.broadcast_to(b_out[None, :], (128, C)))
    bffnB = np.ascontiguousarray(np.broadcast_to(b_ffn[None, :], (128, C)))
    cos_t, sin_t = _rope_tables(t_seq)
    kk = np.arange(128)[:, None]
    qq = np.arange(128)[None, :]
    return {
        "wqkvT": wqkvT, "woutT": woutT, "wffnT": wffnT,
        "boutB": boutB, "bffnB": bffnB,
        "ropecos": cos_t, "ropesin": sin_t,
        "mask3": np.concatenate([
            np.where(kk >= qq, 0.0, -1e5),
            np.zeros((128, 128)),
            np.where(kk <= qq, 0.0, -1e5)], axis=1).astype(ml_dtypes.bfloat16),
        "identbf": np.eye(128, dtype=ml_dtypes.bfloat16),
    }


def kernel(L0_attn_x, L0_attn_w_qkv, L0_attn_w_out, L0_attn_b_out,
           L0_ffn_weight, L0_ffn_bias):
    x = np.ascontiguousarray(np.asarray(L0_attn_x, dtype=np.float32))
    shared = host_inputs(np.asarray(L0_attn_w_qkv, dtype=np.float32),
                         np.asarray(L0_attn_w_out, dtype=np.float32),
                         np.asarray(L0_attn_b_out, dtype=np.float32),
                         np.asarray(L0_ffn_weight, dtype=np.float32),
                         np.asarray(L0_ffn_bias, dtype=np.float32))
    nc = build_program()
    in_maps = []
    for c in range(N_CORES):
        im = dict(shared)
        im["x"] = np.ascontiguousarray(
            x[c * B_CORE:(c + 1) * B_CORE].reshape(B_CORE * T_SEQ, C))
        in_maps.append(im)
    res = run_bass_kernel_spmd(nc, in_maps, core_ids=list(range(N_CORES)))
    out = np.concatenate([r["out"].reshape(B_CORE, T_SEQ, C) for r in res.results], axis=0)
    return out.astype(np.float32)


if __name__ == "__main__":
    build_program()
    print("built ok")


# version 42
# speedup vs baseline: 1.1007x; 1.0021x over previous
import numpy as np
import ml_dtypes

import concourse.bass as bass
import concourse.mybir as mybir
from concourse.bass_utils import run_bass_kernel_spmd

F32 = mybir.dt.float32
BF16 = mybir.dt.bfloat16
F8 = mybir.dt.float8e4
AF = mybir.ActivationFunctionType
ALU = mybir.AluOpType

N_CORES = 8
B_FULL = 16
T_SEQ = 2048
C = 1024
H = 16
D = 64
QT = 128
EPS = 1e-5
RING = 4

B_CORE = B_FULL // N_CORES
NT = T_SEQ // QT


def _qkv_row_perm():
    rows = []
    for base, par in ((0, 0), (0, 1), (64, 0), (64, 1)):
        for h in range(H):
            for j in range(32):
                rows.append(h * 192 + base + 2 * j + par)
    for h in range(H):
        for d in range(D):
            rows.append(h * 192 + 128 + d)
    assert len(rows) == 3 * C and len(set(rows)) == 3 * C
    return np.asarray(rows, dtype=np.int64)


def _rope_tables(t_seq):
    d2 = D // 2
    theta = (10000.0 ** (-(np.arange(d2, dtype=np.float32) / np.float32(d2)))).astype(np.float32)
    ang32 = np.arange(t_seq, dtype=np.float32)[:, None] * theta[None, :]
    cosv = np.cos(ang32.astype(np.float64)).astype(np.float32)
    sinv = np.sin(ang32.astype(np.float64)).astype(np.float32)
    p = np.arange(128) % d2
    return (np.ascontiguousarray(cosv[:, p].T).astype(ml_dtypes.bfloat16),
            np.ascontiguousarray(sinv[:, p].T).astype(ml_dtypes.bfloat16))


class _Builder:
    def __init__(self, b_core, nt, debug=False):
        self.debug = debug
        self.b_core, self.nt = b_core, nt
        self.t_seq = nt * QT
        self.n_tiles = b_core * nt
        self.queues = {k: [] for k in ("sync", "gpsimd", "vector", "scalar", "tensor")}
        self.counts = {k: 0 for k in ("x", "tx", "ty", "wg", "wh", "wi", "wj", "wo", "wf", "st", "t", "v", "s", "g")}
        self.miles = {}
        self.sems = {}

    def emit(self, eng, fn):
        self.queues[eng].append(fn)

    def wait(self, eng, milestone):
        sem_name, val = self.miles[milestone]
        self.emit(eng, lambda e, sn=sem_name, v=val: e.wait_ge(self.sems[sn], v))

    def step(self, eng, sem_name, milestone, fn, by=1):
        def run(e, f=fn, sn=sem_name, b=by):
            ins = f(e)
            ins.then_inc(self.sems[sn], b)
        self.emit(eng, run)
        self.counts[sem_name] += by
        if milestone is not None:
            self.miles[milestone] = (sem_name, self.counts[sem_name])

    def dma_step(self, eng, sem_name, milestone, fn):
        prev = self.counts[sem_name]
        if prev:
            self.emit(eng, lambda e, sn=sem_name, v=prev: e.wait_ge(self.sems[sn], v))
        self.step(eng, sem_name, milestone, fn, by=16)

    def build(self):
        nc = bass.Bass("TRN2", target_bir_lowering=False)
        self.nc = nc
        t_seq, nt = self.t_seq, self.nt

        self.x_h = nc.dram_tensor("x", [self.b_core * t_seq, C], F32, kind="ExternalInput")
        self.wqkv_h = nc.dram_tensor("wqkvT", [C, 3 * C], F32, kind="ExternalInput")
        self.wout_h = nc.dram_tensor("woutT", [C, C], F32, kind="ExternalInput")
        self.wffn_h = nc.dram_tensor("wffnT", [C, C], F32, kind="ExternalInput")
        self.bout_h = nc.dram_tensor("boutB", [128, C], F32, kind="ExternalInput")
        self.bffn_h = nc.dram_tensor("bffnB", [128, C], F32, kind="ExternalInput")
        self.cos_h = nc.dram_tensor("ropecos", [128, t_seq], BF16, kind="ExternalInput")
        self.sin_h = nc.dram_tensor("ropesin", [128, t_seq], BF16, kind="ExternalInput")
        self.mask3_h = nc.dram_tensor("mask3", [128, 384], BF16, kind="ExternalInput")
        self.idbf_h = nc.dram_tensor("identbf", [128, 128], BF16, kind="ExternalInput")
        self.out_h = nc.dram_tensor("out", [self.b_core * t_seq, C], F32, kind="ExternalOutput")

        from contextlib import ExitStack
        es = ExitStack()
        sbuf = lambda n, sh, dt: es.enter_context(nc.sbuf_tensor(n, sh, dt))

        self.wqkvT = sbuf("wqkvT_sb", [128, 8 * 3 * C], BF16)
        self.woutT = sbuf("woutT_sb", [128, 8 * C], BF16)
        self.wffnT = sbuf("wffnT_sb", [128, 8 * C], BF16)
        self.wf8 = sbuf("wf8_sb", [128, C], F8)
        self.wstg = [sbuf(f"wstg{i}", [128, C], F32) for i in range(4)]
        self.cos_sb = sbuf("cos_sb", [128, t_seq], BF16)
        self.sin_sb = sbuf("sin_sb", [128, t_seq], BF16)
        self.boutB = sbuf("boutB_sb", [128, C], F32)
        self.bffnB = sbuf("bffnB_sb", [128, C], F32)
        self.mask3 = sbuf("mask3_sb", [128, 384], BF16)
        self.idbf = sbuf("idbf_sb", [128, 128], BF16)
        self.x_in = [sbuf(f"x_in{i}", [128, C], F32) for i in range(2)]
        self.x3_out = [sbuf(f"x3_out{i}", [128, C], F32) for i in range(2)]
        self.ln1_bf = sbuf("ln1_bf", [128, C], BF16)
        self.ln1T = sbuf("ln1T", [128, C], BF16)
        self.qraw = [sbuf(f"qraw{i}", [128, 512], BF16) for i in range(4)]
        self.rp_t = [sbuf(f"rp_t{i}", [128, 512], BF16) for i in range(2)]
        self.qrx = [sbuf(f"qrx{i}", [128, 512], BF16) for i in range(2)]
        self.krx = [sbuf(f"krx{i}", [128, RING * 512], BF16) for i in range(2)]
        self.vaug = sbuf("vaug", [128, RING * 16 * 65], BF16)
        self.expT = sbuf("expT", [128, 3 * H * 128], BF16)
        self.attn_out = sbuf("attn_out", [128, C], BF16)
        self.attn_outT = sbuf("attn_outT", [128, C], BF16)
        self.rd_sb = sbuf("rd_sb", [128, H], F32)
        self.proj_b = sbuf("proj_b", [128, C], F32)
        self.t_pa = sbuf("t_pa", [128, C], F32)
        self.x2_sb = sbuf("x2_sb", [128, C], F32)
        self.ln2_f8 = sbuf("ln2_f8", [128, C], F8)
        self.ln2_bf = sbuf("ln2_bf", [128, C], BF16)
        self.ln2T = sbuf("ln2T_bf", [128, C], BF16)
        self.ffn_b = sbuf("ffn_b", [128, C], F32)
        self.t_ffn = sbuf("t_ffn", [128, C], F32)
        self.sq_scr = sbuf("sq_scr", [128, C], BF16)
        self.lnscr = sbuf("lnscr", [128, 4 * 2 * 8], F32)
        self.dbg_expin = sbuf("dbg_expin", [128, 3 * 128], F32)

        self.ps_all = es.enter_context(nc.psum_tensor("ps_all", [128, 4096], F32))
        for n in ("x", "tx", "ty", "wg", "wh", "wi", "wj", "wo", "wf", "st", "t", "v", "s", "g"):
            self.sems[n] = es.enter_context(nc.semaphore(f"sem_{n}"))

        c0 = self._ctx(0)
        self._prologue(c0)
        self._emit_qkv(c0)
        self._emit_rope(c0)
        self._pump(4)
        for gi in range(self.n_tiles):
            c = self._ctx(gi)
            nxt = self._ctx(gi + 1) if gi + 1 < self.n_tiles else None
            self._emit_attention(c, nxt)
            self._emit_tail(c, nxt)
        if self.debug:
            self._emit_debug_taps()
        total_st = self.counts["st"]
        self.emit("gpsimd", lambda e, v=total_st: e.wait_ge(self.sems["st"], v))

        with es, nc.Block() as block:
            @block.sync
            def _(eng):
                for fn in self.queues["sync"]:
                    fn(eng)

            @block.gpsimd
            def _(eng):
                for fn in self.queues["gpsimd"]:
                    fn(eng)

            @block.vector
            def _(eng):
                for fn in self.queues["vector"]:
                    fn(eng)

            @block.scalar
            def _(eng):
                for fn in self.queues["scalar"]:
                    fn(eng)

            @block.tensor
            def _(eng):
                for fn in self.queues["tensor"]:
                    fn(eng)
        return nc

    def bank(self, b, lo=0, hi=512):
        return self.ps_all[:, b * 512 + lo: b * 512 + hi]

    def _prologue(self, c0):
        step, wait = self.step, self.wait

        def ld(eng, sem, dst, src, mile=None):
            self.dma_step(eng, sem, mile,
                 lambda e, d=dst, s=src: e.dma_start(out=d[:], in_=s[:, :]))
        ld("sync", "x", self.x_in[0], self.x_h[0:128], "x_tile0")
        if self.n_tiles > 1:
            ld("sync", "x", self.x_in[1], self.x_h[128:256], "x_tile1")
        ln1_done0 = self._emit_ln1(c0)
        self._emit_ln1T(c0, ln1_done0)
        for dst, src in ((self.cos_sb, self.cos_h), (self.sin_sb, self.sin_h),
                         (self.idbf, self.idbf_h), (self.mask3, self.mask3_h)):
            ld("scalar", "wh", dst, src)
        self.miles["tables"] = ("wh", self.counts["wh"])

        jobs = []
        for hf in range(3):
            for cch in range(8):
                src = self.wqkv_h[cch * 128:(cch + 1) * 128, hf * C:(hf + 1) * C]
                jobs.append(("gpsimd", f"wqkv{cch}_{hf}", src, cch * 3 * C + hf * C))
        qsems = {"gpsimd": ("wg", "wi"), "sync": ("wh", "wj")}
        qn = {"gpsimd": 0, "sync": 0}
        for idx, (q, name, src, off) in enumerate(jobs):
            if idx >= 4:
                wait(q, f"wcast{idx - 4}")
            sem = qsems[q][qn[q] % 2]
            qn[q] += 1
            self.dma_step(q, sem, name,
                          lambda e, s_=src, b_=idx % 4: e.dma_start(
                              out=self.wstg[b_][:, :], in_=s_))
            wait("vector", name)

            def v_cast(e, o_=off, b_=idx % 4):
                e.drain()
                return e.tensor_copy(self.wqkvT[:, o_:o_ + C], self.wstg[b_][:, :])
            step("vector", "v", f"wcast{idx}", v_cast)
        sem = qsems["gpsimd"][qn["gpsimd"] % 2]
        self.dma_step("gpsimd", sem, "boutld",
                      lambda e: e.dma_start(out=self.boutB[:], in_=self.bout_h[:, :]))
        sem = qsems["gpsimd"][(qn["gpsimd"] + 1) % 2]
        self.dma_step("gpsimd", sem, "bffnld",
                      lambda e: e.dma_start(out=self.bffnB[:], in_=self.bffn_h[:, :]))

        def v_prologue(e):
            ones = self.vaug[:].rearrange("p (r h c) -> p r h c", r=RING, h=16)
            return e.memset(ones[:, :, :, 64:65], 1.0)
        step("vector", "v", "v_prologue", v_prologue)

        def mk_load_wo(m):
            def go():
                self.wait("sync", f"wcast{20 + m}" if m < 4 else f"wocast{m - 4}")
                sem = ("wh", "wj")[m % 2]
                self.dma_step("sync", sem, f"wo{m}",
                              lambda e, m=m: e.dma_start(
                                  out=self.wstg[m % 4][:, :],
                                  in_=self.wout_h[m * 128:(m + 1) * 128, :]))
            return go

        def mk_cast_wo(m):
            def go():
                self.wait("vector", f"wo{m}")

                def v_c(e, m=m):
                    e.drain()
                    return e.tensor_copy(self.woutT[:, m * C:(m + 1) * C],
                                         self.wstg[m % 4][:, :])
                self.step("vector", "v", f"wocast{m}", v_c)
            return go

        def mk_load_wf(m):
            def go():
                self.wait("gpsimd", f"wocast{4 + m}" if m < 4 else f"wfcast{m - 4}")
                sem = ("wg", "wi")[m % 2]
                self.dma_step("gpsimd", sem, f"wf{m}",
                              lambda e, m=m: e.dma_start(
                                  out=self.wstg[m % 4][:, :],
                                  in_=self.wffn_h[m * 128:(m + 1) * 128, :]))
            return go

        def mk_cast_wf(m):
            def go():
                self.wait("vector", f"wf{m}")

                def v_c(e, m=m):
                    e.drain()
                    e.tensor_copy(self.wf8[:], self.wstg[m % 4][:, :])
                    e.drain()
                    return e.tensor_copy(self.wffnT[:, m * C:(m + 1) * C], self.wf8[:])
                self.step("vector", "v", f"wfcast{m}", v_c)
            return go

        p = [mk_load_wo(0), mk_load_wo(1)]
        for m in range(2, 8):
            p += [mk_cast_wo(m - 2), mk_load_wo(m)]
        p += [mk_cast_wo(6), mk_cast_wo(7), mk_load_wf(0), mk_load_wf(1)]
        for m in range(2, 8):
            p += [mk_cast_wf(m - 2), mk_load_wf(m)]
        p += [mk_cast_wf(6), mk_cast_wf(7)]
        self.pending = p

    def _pump(self, n):
        while n and self.pending:
            self.pending.pop(0)()
            n -= 1

    def _emit_debug_taps(self):
        tensors = {
            "ln1_bf": self.ln1_bf, "ln1T": self.ln1T,
            "qraw0": self.qraw[0], "qraw1": self.qraw[1],
            "qraw2": self.qraw[2], "qraw3": self.qraw[3],
            "qrx0": self.qrx[0], "qrx1": self.qrx[1],
            "krx0": self.krx[0], "krx1": self.krx[1],
            "vaug": self.vaug, "expT": self.expT,
            "attn_out": self.attn_out, "attn_outT": self.attn_outT,
            "proj_b": self.proj_b, "t_pa": self.t_pa, "x2_sb": self.x2_sb,
            "ln2_f32": self.ln2_f8, "ln2T_f8": self.ln2T, "ffn_b": self.ffn_b,
            "t_ffn": self.t_ffn,
        }
        for name in ("t", "v", "s", "g"):
            self.emit("gpsimd",
                      lambda e, sn=name, v=self.counts[name]: e.wait_ge(self.sems[sn], v))
        for name, t in tensors.items():
            h = self.nc.dram_tensor("dbg_" + name, list(t.shape), t.dtype, kind="ExternalOutput")
            self.dma_step("gpsimd", "st", None,
                          lambda e, hh=h, tt=t: e.dma_start(out=hh[:, :], in_=tt[:]))

    def _emit_ln(self, gi, phase, src_sb, dst_sb, s1_pre=None):
        step, wait = self.step, self.wait
        base = (phase * 2 + gi % 2) * 8
        L = self.lnscr
        s1 = L[:, base + 0: base + 1]
        mean = L[:, base + 1: base + 2]
        msq = L[:, base + 2: base + 3]
        var = L[:, base + 3: base + 4]
        rin = L[:, base + 4: base + 5]
        rstd = L[:, base + 5: base + 6]
        nm = L[:, base + 6: base + 7]
        s2 = L[:, base + 7: base + 8]
        tag = f"ln{phase}_{gi}"

        def s_sq(e):
            return e.activation(self.sq_scr[:], src_sb[:], AF.Square, accum_out=s2)
        step("scalar", "s", f"{tag}_sq", s_sq)

        if s1_pre is None:
            def v_head(e):
                e.reduce_sum(s1, src_sb[:], axis=mybir.AxisListType.X)
                e.drain()
                return e.nop()
            self.emit("vector", v_head)
        else:
            wait("vector", s1_pre)

        def v_stats(e):
            e.tensor_scalar_mul(mean, s1, 1.0 / C)
            e.tensor_scalar(msq, s1, s1, 1.0 / (C * C), ALU.mult, ALU.mult)
            return e.nop()
        self.emit("vector", v_stats)
        wait("vector", f"{tag}_sq")

        def v_stats2(e):
            e.tensor_scalar(var, s2, 1.0 / C, EPS, ALU.mult, ALU.add)
            e.drain()
            return e.tensor_scalar(var, var, msq, None, ALU.subtract)
        step("vector", "v", f"{tag}_rin", v_stats2)

        wait("scalar", f"{tag}_rin")

        def s_rstd(e):
            e.activation(rin, var, AF.Ln)
            e.drain()
            return e.activation(rstd, rin, AF.Exp, scale=-0.5)
        step("scalar", "s", f"{tag}_rstd", s_rstd)

        wait("vector", f"{tag}_rstd")

        def v_nm(e):
            e.drain()
            return e.tensor_scalar(nm, mean, rstd, -1.0, ALU.mult, ALU.mult)
        step("vector", "v", f"{tag}_nm", v_nm)

        def v_norm(e):
            e.drain()
            return e.tensor_scalar(dst_sb[:], src_sb[:], rstd, nm, ALU.mult, ALU.add)
        step("vector", "v", f"{tag}_done", v_norm)
        return f"{tag}_done"

    def _ctx(self, gi):
        b, m = divmod(gi, self.nt)
        kcs = [j for j in (m - 2, m - 1, m) if j >= 0]
        return {
            "gi": gi, "b": b, "m": m, "kcs": kcs, "nkc": len(kcs),
            "jr0": 3 - len(kcs),
            "slots": [(b * self.nt + j) % RING for j in kcs],
            "vslot": (b * self.nt + m) % RING,
            "xbuf": self.x_in[gi % 2],
            "obuf": self.x3_out[gi % 2],
        }

    def _emit_ln1(self, c):
        gi = c["gi"]
        self.wait("vector", f"x_tile{gi}")
        self.wait("scalar", f"x_tile{gi}")
        return self._emit_ln(gi, 0, c["xbuf"], self.ln1_bf)

    def _emit_ln1T(self, c, ln1_done):
        gi = c["gi"]
        self.wait("sync", ln1_done)

        def d_ln1T(e):
            return e.dma_start(out=self.ln1T[:].rearrange("p (c t) -> p c t", c=8),
                               in_=self.ln1_bf[:].rearrange("p (c t) -> p c t", c=8),
                               transpose=True)
        self.dma_step("sync", "tx", f"ln1T_{gi}", d_ln1T)

    def _emit_qkv(self, c):
        gi = c["gi"]
        step, wait, bank = self.step, self.wait, self.bank
        vslot = c["vslot"]
        wait("tensor", f"ln1T_{gi}")
        if gi == 0:
            wait("tensor", "wcast7")

        def mk_qk(cgs, mile):
            def t_qk(e, cgs=cgs):
                for cg in cgs:
                    bnk, off = cg // 4, (cg % 4) * 128
                    for kc in range(8):
                        ins = e.matmul(bank(bnk, off, off + 128),
                                       self.wqkvT[:, kc * 3 * C + cg * 128: kc * 3 * C + (cg + 1) * 128],
                                       self.ln1T[:, kc * 128:(kc + 1) * 128],
                                       start=(kc == 0), stop=(kc == 7))
                return ins
            step("tensor", "t", mile, t_qk)
        if gi == 0:
            mk_qk(range(8), "qkq_0")
            wait("tensor", "wcast15")
            mk_qk(range(8, 16), f"qk_done_{gi}")
            wait("tensor", "wcast23")
        else:
            mk_qk(range(16), f"qk_done_{gi}")

        def t_v(e):
            for nb in range(2):
                for kc in range(8):
                    ins = e.matmul(bank(4 + nb),
                                   self.ln1T[:, kc * 128:(kc + 1) * 128],
                                   self.wqkvT[:, kc * 3 * C + 2048 + nb * 512: kc * 3 * C + 2048 + (nb + 1) * 512],
                                   start=(kc == 0), stop=(kc == 7))
            return ins
        step("tensor", "t", f"v_done_{gi}", t_v)

        wait("scalar", f"qk_done_{gi}")

        def s_qk_evac(e):
            for i in range(4):
                ins = e.activation(self.qraw[i][:], bank(i), AF.Copy, scale=(0.125 if i < 2 else 1.0))
            return ins
        step("scalar", "s", f"qk_evac_{gi}", s_qk_evac)

        wait("scalar", f"v_done_{gi}")

        def s_v_evac(e):
            dstv = self.vaug[:].rearrange("p (r h c) -> p r h c", r=RING, h=16)
            for nb in range(2):
                ins = e.activation(dstv[:, vslot, nb * 8:(nb + 1) * 8, 0:64], bank(4 + nb), AF.Copy)
            return ins
        step("scalar", "s", f"v_evac_{gi}", s_v_evac)

    def _emit_rope(self, c):
        gi, m, vslot = c["gi"], c["m"], c["vslot"]
        step, wait = self.step, self.wait
        wait("vector", f"qk_evac_{gi}")
        if gi == 0:
            wait("vector", "tables")
        tcos = self.cos_sb[:, m * 128:(m + 1) * 128].unsqueeze(1).broadcast_to([128, 4, 128])
        tsin = self.sin_sb[:, m * 128:(m + 1) * 128].unsqueeze(1).broadcast_to([128, 4, 128])

        def v_rope(e):
            g4 = lambda ap: ap.rearrange("p (g t) -> p g t", g=4)
            for qk in range(2):
                x1 = g4(self.qraw[2 * qk][:])
                x2 = g4(self.qraw[2 * qk + 1][:])
                if qk == 0:
                    o1, o2 = g4(self.qrx[0][:]), g4(self.qrx[1][:])
                else:
                    o1 = g4(self.krx[0][:, vslot * 512:(vslot + 1) * 512])
                    o2 = g4(self.krx[1][:, vslot * 512:(vslot + 1) * 512])
                t1, t2 = g4(self.rp_t[0][:]), g4(self.rp_t[1][:])
                e.drain()
                e.tensor_tensor(t1, x1, tcos, ALU.mult)
                e.tensor_tensor(t2, x2, tsin, ALU.mult)
                e.drain()
                e.tensor_tensor(o1, t1, t2, ALU.subtract)
                e.drain()
                e.tensor_tensor(t1, x1, tsin, ALU.mult)
                e.tensor_tensor(t2, x2, tcos, ALU.mult)
                e.drain()
                ins = e.tensor_tensor(o2, t1, t2, ALU.add)
            return ins
        step("vector", "v", f"rope_{gi}", v_rope)

    SC_BANKS = [[0, 1, 2, 3], [4, 5, 6, 7], [5, 1, 2, 3], [7, 1, 2, 3]]
    AV_BANKS = [0, 4, 6, 5]

    def _emit_attention(self, c, nxt=None):
        gi, nkc, jr0, slots = c["gi"], c["nkc"], c["jr0"], c["slots"]
        step, wait, bank = self.step, self.wait, self.bank
        self.ln1_done_nxt = None
        expT_v = self.expT[:].rearrange("p (j h t) -> p j h t", j=3, h=H)
        vaug_v = self.vaug[:].rearrange("p (r hc) -> p r hc", r=RING)

        def emit_scores_head(w, i):
            h = 4 * w + i
            if w == 0:
                if h == 0:
                    wait("tensor", f"rope_{gi}")
                    wait("tensor", f"v_evac_{gi}")
                    if gi == 0:
                        wait("tensor", "v_prologue")
                        wait("tensor", "tables")
                if gi > 0:
                    wait("tensor", f"exphead_{gi - 1}_{12 + i}")
            else:
                wait("tensor", f"exphead_{gi}_{h - 4}")

            def t_sc(e, h=h, i=i):
                strip = i * 32
                g = h // 4
                e.matmul(bank(i, 0, nkc * 128), self.idbf[:],
                         self.mask3[:, jr0 * 128:384], start=True, stop=False)
                for jr in range(nkc):
                    sl = slots[jr]
                    for par in range(2):
                        ins = e.matmul(bank(i, jr * 128, (jr + 1) * 128),
                                       self.krx[par][strip:strip + 32,
                                                     sl * 512 + g * 128: sl * 512 + (g + 1) * 128],
                                       self.qrx[par][strip:strip + 32, g * 128:(g + 1) * 128],
                                       start=False, stop=(par == 1 and jr == nkc - 1),
                                       tile_position=(strip, 0))
                return ins
            step("tensor", "t", f"schead_{gi}_{h}", t_sc)

            wait("scalar", f"schead_{gi}_{h}")

            def s_exp(e, h=h, i=i):
                return e.activation(expT_v[:, jr0:3, h, :], bank(i, 0, nkc * 128), AF.Exp)
            step("scalar", "s", f"exphead_{gi}_{h}", s_exp)

        def emit_av(w):
            heads = tuple(range(4 * w, 4 * w + 4))
            if w >= 2:
                wait("tensor", f"aoev_{gi}_w{w - 2}")
            for h in range(4 * w, 4 * w + 4):
                wait("tensor", f"exphead_{gi}_{h}")

                def t_av(e, w=w, h=h):
                    ab = 4 + (w % 2)
                    off = (h % 4) * 128
                    for jr in range(nkc):
                        ins = e.matmul(bank(ab, off, off + 65),
                                       expT_v[:, jr0 + jr, h, :],
                                       vaug_v[:, slots[jr], h * 65:(h + 1) * 65],
                                       start=(jr == 0), stop=(jr == nkc - 1))
                    return ins
                step("tensor", "t",
                     f"av_{gi}_w{w}" if h == 4 * w + 3 else None, t_av)

            wait("vector", f"av_{gi}_w{w}")

            def v_div(e, w=w, heads=heads):
                ab = 4 + (w % 2)
                for h in heads:
                    off = (h % 4) * 128
                    e.reciprocal(self.rd_sb[:, h:h + 1], bank(ab, off + 64, off + 65))
                e.drain()
                for h in heads:
                    off = (h % 4) * 128
                    ins = e.tensor_scalar(self.attn_out[:, h * 64:(h + 1) * 64],
                                          bank(ab, off, off + 64),
                                          self.rd_sb[:, h:h + 1], None, ALU.mult)
                return ins
            step("vector", "v", f"div_{gi}_w{w}", v_div)
            if gi == 0:
                self._pump(4)

        def emit_trans_proj(w):
            wait("tensor", f"div_{gi}_w{w}")
            abbf = bank(4 + (w % 2)).bitcast(BF16)

            def t_tr(e, w=w):
                for k in range(2):
                    cc = 2 * w + k
                    ins = e.transpose(abbf[:, k * 128:(k + 1) * 128],
                                      self.attn_out[:, cc * 128:(cc + 1) * 128], self.idbf[:])
                return ins
            step("tensor", "t", f"aotr_{gi}_w{w}", t_tr)

            wait("scalar", f"aotr_{gi}_w{w}")

            def s_ev(e, w=w):
                return e.activation(self.attn_outT[:, w * 256:(w + 1) * 256],
                                    abbf[:, 0:256], AF.Copy)
            step("scalar", "s", f"aoev_{gi}_w{w}", s_ev)

            wait("tensor", f"aoev_{gi}_w{w}")
            if gi == 0:
                wait("tensor", f"wocast{2 * w + 1}")

            def t_proj(e, w=w):
                for nb in range(2):
                    for k in range(2):
                        cc = 2 * w + k
                        ins = e.matmul(bank(6 + nb),
                                       self.attn_outT[:, cc * 128:(cc + 1) * 128],
                                       self.woutT[:, cc * C + nb * 512: cc * C + (nb + 1) * 512],
                                       start=(w == 0 and k == 0), stop=(w == 3 and k == 1))
                return ins
            mile = f"proj_{gi}" if w == 3 else None
            step("tensor", "t", mile or f"projw_{gi}_w{w}", t_proj)

        for i in range(4):
            emit_scores_head(0, i)
        for i in range(4):
            emit_scores_head(1, i)
        emit_av(0)
        if nxt is not None:
            self.ln1_done_nxt = self._emit_ln1(nxt)
        for i in range(4):
            emit_scores_head(2, i)
        emit_av(1)
        emit_trans_proj(0)
        for i in range(4):
            emit_scores_head(3, i)
        emit_av(2)
        emit_trans_proj(1)
        emit_av(3)
        emit_trans_proj(2)
        emit_trans_proj(3)

    def _emit_tail(self, c, nxt=None):
        gi, xbuf, obuf = c["gi"], c["xbuf"], c["obuf"]
        step, wait, emit, bank = self.step, self.wait, self.emit, self.bank

        wait("vector", f"proj_{gi}")
        s1_pa = self.lnscr[:, (1 * 2 + gi % 2) * 8: (1 * 2 + gi % 2) * 8 + 1]

        if gi == 0:
            wait("vector", "boutld")

        def v_proj_b(e):
            return e.scalar_tensor_tensor(self.proj_b[:], self.ps_all[:, 6 * 512: 8 * 512], 1.0,
                                          self.boutB[:], ALU.mult, ALU.add, accum_out=s1_pa)
        step("vector", "v", f"proj_b_{gi}", v_proj_b)
        if gi == 0:
            self._pump(6)

        if nxt:
            self._emit_ln1T(nxt, self.ln1_done_nxt)

        wait("scalar", f"proj_b_{gi}")
        tpa_done = self._emit_ln(gi, 1, self.proj_b, self.t_pa, s1_pre=f"proj_b_{gi}")

        s1_x2 = self.lnscr[:, (2 * 2 + gi % 2) * 8: (2 * 2 + gi % 2) * 8 + 1]
        wait("vector", tpa_done)
        if gi > 0:
            wait("vector", f"x3_{gi - 1}")

        def v_x2(e):
            return e.scalar_tensor_tensor(self.x2_sb[:], xbuf[:], 1.0, self.t_pa[:],
                                          ALU.mult, ALU.add, accum_out=s1_x2)
        step("vector", "v", f"x2_{gi}", v_x2)
        if gi == 0:
            self._pump(99)

        if gi + 2 < self.n_tiles:
            wait("sync", f"x2_{gi}")
            self.dma_step("sync", "x", f"x_tile{gi + 2}",
                          lambda e, g=gi + 2: e.dma_start(out=self.x_in[g % 2][:],
                                                          in_=self.x_h[g * 128:(g + 1) * 128, :]))

        wait("scalar", f"x2_{gi}")
        ln2_done = self._emit_ln(gi, 2, self.x2_sb, self.ln2_f8, s1_pre=f"x2_{gi}")

        def v_ln2bf(e):
            e.drain()
            return e.tensor_copy(self.ln2_bf[:], self.ln2_f8[:])
        step("vector", "v", f"ln2bf_{gi}", v_ln2bf)

        if nxt:
            self._emit_qkv(nxt)
            self._emit_rope(nxt)

        wait("scalar", f"ln2bf_{gi}")

        def d_ln2T_a(e):
            return e.dma_start(out=self.ln2T[:, 0:512].rearrange("p (c t) -> p c t", c=4),
                               in_=self.ln2_bf[:, 0:512].rearrange("p (c t) -> p c t", c=4),
                               transpose=True)
        self.dma_step("scalar", "tx", f"ln2T_{gi}a", d_ln2T_a)

        def d_ln2T_b(e):
            return e.dma_start(out=self.ln2T[:, 512:1024].rearrange("p (c t) -> p c t", c=4),
                               in_=self.ln2_bf[:, 512:1024].rearrange("p (c t) -> p c t", c=4),
                               transpose=True)
        self.dma_step("scalar", "tx", f"ln2T_{gi}b", d_ln2T_b)

        def mm_ffn(e, cch, nb):
            return e.matmul(bank(6 + nb),
                            self.ln2T[:, cch * 128:(cch + 1) * 128],
                            self.wffnT[:, cch * C + nb * 512: cch * C + (nb + 1) * 512],
                            start=(cch == 0), stop=(cch == 7))

        wait("tensor", f"ln2T_{gi}a")
        wait("tensor", f"proj_b_{gi}")
        if gi == 0:
            wait("tensor", "wfcast7")

        def t_ffn_a(e):
            for cch in range(4):
                for nb in range(2):
                    ins = mm_ffn(e, cch, nb)
            return ins
        step("tensor", "t", None, t_ffn_a)
        wait("tensor", f"ln2T_{gi}b")

        def t_ffn_b(e):
            for cch in range(4, 8):
                for nb in range(2):
                    ins = mm_ffn(e, cch, nb)
            return ins
        step("tensor", "t", f"ffn_{gi}", t_ffn_b)

        wait("vector", f"ffn_{gi}")
        s1_f = self.lnscr[:, (3 * 2 + gi % 2) * 8: (3 * 2 + gi % 2) * 8 + 1]

        if gi == 0:
            wait("vector", "bffnld")

        def v_ffn_b(e):
            return e.scalar_tensor_tensor(self.ffn_b[:], self.ps_all[:, 6 * 512: 8 * 512], 1.0,
                                          self.bffnB[:], ALU.mult, ALU.add, accum_out=s1_f)
        step("vector", "v", f"ffn_b_{gi}", v_ffn_b)

        wait("scalar", f"ffn_b_{gi}")
        if gi > 0:
            wait("scalar", f"x3_{gi - 1}")
        tf_done = self._emit_ln(gi, 3, self.ffn_b, self.t_ffn, s1_pre=f"ffn_b_{gi}")

        wait("gpsimd", tf_done)
        if gi >= 2:
            emit("gpsimd", lambda e, v=16 * (gi - 1): e.wait_ge(self.sems["st"], v))

        def g_x3(e):
            return e.tensor_tensor(obuf[:], self.x2_sb[:], self.t_ffn[:], ALU.add)
        step("gpsimd", "g", f"x3_{gi}", g_x3)
        wait("gpsimd", f"x3_{gi}")

        def g_store(e):
            return e.dma_start(out=self.out_h[gi * 128:(gi + 1) * 128, :], in_=obuf[:])
        self.dma_step("gpsimd", "st", f"store_{gi}", g_store)


def build_program(b_core=B_CORE, nt=NT, debug=False):
    return _Builder(b_core, nt, debug=debug).build()


DBG_TAPS = ("ln1_bf", "ln1T", "qraw0", "qraw1", "qraw2", "qraw3", "qrx0", "qrx1",
            "krx0", "krx1", "vaug", "expT", "attn_out", "attn_outT", "proj_b",
            "t_pa", "x2_sb", "ln2_f32", "ln2T_f8", "ffn_b", "t_ffn")


def host_inputs(w_qkv, w_out, b_out, w_ffn, b_ffn, t_seq=T_SEQ):
    perm = _qkv_row_perm()
    wqkvT = np.ascontiguousarray(w_qkv[perm].T)
    woutT = np.ascontiguousarray(w_out.T)
    wffnT = np.ascontiguousarray(w_ffn.T)
    boutB = np.ascontiguousarray(np.broadcast_to(b_out[None, :], (128, C)))
    bffnB = np.ascontiguousarray(np.broadcast_to(b_ffn[None, :], (128, C)))
    cos_t, sin_t = _rope_tables(t_seq)
    kk = np.arange(128)[:, None]
    qq = np.arange(128)[None, :]
    return {
        "wqkvT": wqkvT, "woutT": woutT, "wffnT": wffnT,
        "boutB": boutB, "bffnB": bffnB,
        "ropecos": cos_t, "ropesin": sin_t,
        "mask3": np.concatenate([
            np.where(kk >= qq, 0.0, -1e5),
            np.zeros((128, 128)),
            np.where(kk <= qq, 0.0, -1e5)], axis=1).astype(ml_dtypes.bfloat16),
        "identbf": np.eye(128, dtype=ml_dtypes.bfloat16),
    }


def kernel(L0_attn_x, L0_attn_w_qkv, L0_attn_w_out, L0_attn_b_out,
           L0_ffn_weight, L0_ffn_bias):
    x = np.ascontiguousarray(np.asarray(L0_attn_x, dtype=np.float32))
    shared = host_inputs(np.asarray(L0_attn_w_qkv, dtype=np.float32),
                         np.asarray(L0_attn_w_out, dtype=np.float32),
                         np.asarray(L0_attn_b_out, dtype=np.float32),
                         np.asarray(L0_ffn_weight, dtype=np.float32),
                         np.asarray(L0_ffn_bias, dtype=np.float32))
    nc = build_program()
    in_maps = []
    for c in range(N_CORES):
        im = dict(shared)
        im["x"] = np.ascontiguousarray(
            x[c * B_CORE:(c + 1) * B_CORE].reshape(B_CORE * T_SEQ, C))
        in_maps.append(im)
    res = run_bass_kernel_spmd(nc, in_maps, core_ids=list(range(N_CORES)))
    out = np.concatenate([r["out"].reshape(B_CORE, T_SEQ, C) for r in res.results], axis=0)
    return out.astype(np.float32)


if __name__ == "__main__":
    build_program()
    print("built ok")
